# version 05
# baseline (speedup 1.0000x reference)
import dataclasses
import math
from contextlib import ExitStack

import numpy as np
import ml_dtypes

import concourse.bass as bass
import concourse.mybir as mybir
from concourse.bass_utils import run_bass_kernel_spmd
from concourse.expressions_rust import RuntimeValue

F32 = mybir.dt.float32
BF16 = mybir.dt.bfloat16
I32 = mybir.dt.int32
U32 = mybir.dt.uint32
AF = mybir.ActivationFunctionType
ALU = mybir.AluOpType
AX = mybir.AxisListType

D = 4096
S = 4096
NH = 32
NKV = 8
HD = 128
ROWS = 1024
NSLOT = 8
NCH = 32
SHIFT = 20.0
ISQ = 1.0 / math.sqrt(HD)
EPS = 1e-5
f32 = np.float32
KB = 1024


class Ctr:
    def __init__(self, sem):
        self.sem = sem
        self.n = 0


def build_nc(stop_after=None):
    nc = bass.Bass("TRN2", target_bir_lowering=False, debug=False)

    def din(name, shape, dt=F32):
        return nc.dram_tensor(name, shape, dt, kind="ExternalInput").ap()

    x_kv = din("x_kv", [S, D])
    x_q = din("x_q", [ROWS, D])
    qw_q = din("qw_q", [D, NH * HD], I32)
    qw_k = din("qw_k", [D, NKV * HD], I32)
    qw_v = din("qw_v", [D, NKV * HD], I32)
    qw_o = din("qw_o", [NH * HD, D], I32)
    sc_q = din("sc_q", [NCH, NH * HD])
    sc_k = din("sc_k", [NCH, NKV * HD])
    sc_v = din("sc_v", [NCH, NKV * HD])
    sc_o = din("sc_o", [NCH, D])
    wbT = din("wbT", [128, 16 * NCH * 256])
    bb = din("bb", [1, D])
    wrT = din("wrT", [128, NCH * 8])
    aT = din("aT", [128, NCH * 128])
    bstack = din("bstack", [128, D])
    cos_kv = din("cos_kv", [HD, S])
    sin_kv = din("sin_kv", [HD, S])
    cos_q = din("cos_q", [HD, ROWS])
    sin_q = din("sin_q", [HD, ROWS])
    masks = din("masks", [4, 128, 512], BF16)
    iden = din("iden", [128, 128], BF16)
    iden32 = din("iden32", [128, 128])
    e2m = din("e2m", [8, 128])
    ones_r = din("ones_r", [1, 128], BF16)
    ones32_r = din("ones32_r", [1, 128])
    mypid = din("mypid", [1, 1], U32)

    y = nc.dram_tensor("y", [ROWS, D], F32, kind="ExternalOutput").ap()

    xlt_d4 = [nc.dram_tensor(f"xlt_d{i}", [8, 128, S], BF16, kind="Internal")
              for i in range(4)]

    class _XltView:
        def __getitem__(self, idx):
            ch = idx[0] if isinstance(idx, tuple) else idx
            rest = idx[1:] if isinstance(idx, tuple) else (slice(None), slice(None))
            return xlt_d4[ch // 8][(ch % 8,) + tuple(rest)]
    xlt_d = _XltView()
    q_d = nc.dram_tensor("q_d", [NH, HD, ROWS], BF16, kind="Internal")
    k_d = nc.dram_tensor("k_d", [NKV, HD, S], BF16, kind="Internal")
    v_d = nc.dram_tensor("v_d", [NKV, 128, NCH, HD], BF16, kind="Internal")
    x1_d = nc.dram_tensor("x1_d", [ROWS, D], BF16, kind="Internal")

    st = ExitStack()
    E = st.enter_context

    iden_sb = E(nc.sbuf_tensor("iden_sb", [128, 128], BF16))
    iden32_sb = E(nc.sbuf_tensor("iden32_sb", [128, 128], F32))
    ones_sb = E(nc.sbuf_tensor("ones_sb", [1, 128], BF16))
    ones32_sb = E(nc.sbuf_tensor("ones32_sb", [1, 128], F32))
    e2_sb = E(nc.sbuf_tensor("e2_sb", [8, 128], F32))
    masks_sb = E(nc.sbuf_tensor("masks_sb", [128, 4 * 512], BF16))
    wrT_sb = E(nc.sbuf_tensor("wrT_sb", [128, NCH * 8], BF16))
    pid_sb = E(nc.sbuf_tensor("pid_sb", [1, 1], U32))
    statv = E(nc.sbuf_tensor("statv", [128, 16], F32))
    wint = E(nc.sbuf_tensor("wint", [128, 4 * 512], I32))
    scb = E(nc.sbuf_tensor("scb", [128, 2 * 1024], F32))
    ev = E(nc.sbuf_tensor("ev", [128, 4 * 512], BF16))
    rt1 = E(nc.sbuf_tensor("rt1", [128, 512], F32))
    rt2 = E(nc.sbuf_tensor("rt2", [128, 512], F32))
    ssum = E(nc.sbuf_tensor("ssum", [128, 512], F32))
    lrow = E(nc.sbuf_tensor("lrow", [1, 512], F32))
    linvb = E(nc.sbuf_tensor("linvb", [128, 512], F32))
    linv = E(nc.sbuf_tensor("linv", [128, 8], F32))
    xqst = E(nc.sbuf_tensor("xqst", [128, 2 * 512], F32))
    hw = E(nc.sbuf_tensor("hw", [128, ROWS], BF16))
    wbc = E(nc.sbuf_tensor("wbc", [128, ROWS], BF16))
    lgs = E(nc.sbuf_tensor("lgs", [128, 64], F32))
    wfb = E(nc.sbuf_tensor("wfb", [128, 8], BF16))
    wtr = E(nc.sbuf_tensor("wtr", [8, ROWS], F32))
    outst = E(nc.sbuf_tensor("outst", [128, 2 * 512], F32))
    x1st = E(nc.sbuf_tensor("x1st", [128, 2 * 512], BF16))
    evst = E(nc.sbuf_tensor("evst", [128, 2 * 512], F32))

    ARN = E(nc.sbuf_tensor("ARN", [128, 136 * KB], mybir.dt.uint8))

    def av(off_kb, size_kb, dt):
        return ARN[:, off_kb * KB:(off_kb + size_kb) * KB].bitcast(dt)

    xtile = av(0, 32, F32)
    xln = av(32, 16, BF16)
    sqs = av(48, 16, F32)
    slabT = av(64, 32, BF16)
    cosb = av(96, 16, F32)
    sinb = av(112, 16, F32)
    cqb = av(128, 4, F32)
    sqb = av(132, 4, F32)
    wkv = av(0, 64, BF16)
    slab = av(64, 32, BF16)
    xqlt = av(0, 64, BF16)
    wq2 = av(64, 32, BF16)
    ctx = av(0, 64, BF16)
    pstr = av(64, 9, BF16)
    qrhs = av(73, 2, BF16)
    kgb = av(96, 16, BF16)
    vgb = av(112, 16, BF16)
    wo2 = av(64, 32, BF16)
    x1lt = av(64, 64, BF16)
    xtile6 = av(0, 32, F32)
    xln6 = av(32, 16, BF16)
    sqs6 = av(48, 16, F32)
    wb2 = av(0, 32, BF16)
    aT_sb = av(32, 8, BF16)
    bstk_sb = av(40, 8, BF16)
    bb_sb = av(48, 8, BF16)

    psA = E(nc.psum_tensor("psA", [128, 1536], F32))
    psB = E(nc.psum_tensor("psB", [128, 1536], F32))
    psC = E(nc.psum_tensor("psC", [128, 512], F32))
    psT = E(nc.psum_tensor("psT", [128, 512], BF16))

    sLd = Ctr(E(nc.semaphore("sLd")))
    sCst = Ctr(E(nc.semaphore("sCst")))
    sG = Ctr(E(nc.semaphore("sG")))
    sCv = Ctr(E(nc.semaphore("sCv")))
    sSc = Ctr(E(nc.semaphore("sSc")))
    sMM = Ctr(E(nc.semaphore("sMM")))
    sEv = Ctr(E(nc.semaphore("sEv")))
    sVe = Ctr(E(nc.semaphore("sVe")))
    sSt = Ctr(E(nc.semaphore("sSt")))

    sync, gp, te, sc, ve = nc.sync, nc.gpsimd, nc.tensor, nc.scalar, nc.vector

    def dma(eng, ctr, out, in_):
        eng.wait_ge(ctr.sem, ctr.n)
        eng.dma_start(out=out, in_=in_).then_inc(ctr.sem, 16)
        ctr.n += 16

    def inc(eng, ctr, inst=None):
        (inst if inst is not None else eng.nop()).then_inc(ctr.sem, 1)
        ctr.n += 1

    def vsync(inst):
        inst.then_inc(sCv.sem, 1)
        sCv.n += 1
        ve.wait_ge(sCv.sem, sCv.n)

    dma(sync, sCst, iden_sb[:], iden)
    dma(sync, sCst, iden32_sb[:], iden32)
    dma(sync, sCst, ones_sb[:], ones_r)
    dma(sync, sCst, ones32_sb[:], ones32_r)
    dma(sync, sCst, e2_sb[:], e2m)
    dma(sync, sCst, masks_sb[:, :].rearrange("p (m n) -> p m n", n=512),
        masks.rearrange("m p n -> p m n"))
    dma(sync, sCst, cosb, cos_kv)
    dma(sync, sCst, sinb, sin_kv)
    dma(sync, sCst, cqb, cos_q)
    dma(sync, sCst, sqb, sin_q)
    dma(sync, sCst, pid_sb[:], mypid)
    dma(gp, sCst, wrT_sb[:], wrT)
    const_n1 = sCst.n

    gp.wait_ge(sCst.sem, const_n1)
    preg = gp.alloc_register("preg")
    gp.load(preg, pid_sb[0:1, 0:1])
    cpreg = gp.alloc_register("cpreg")
    gp.reg_alu(cpreg, bass.RegisterHandles(preg), 4, ALU.mod)
    offreg = gp.alloc_register("offreg")
    gp.reg_alu(offreg, bass.RegisterHandles(cpreg), 128, ALU.mult)
    offv = gp.snap(RuntimeValue(offreg), min_val=0, max_val=384,
                   guaranteed_mod_val=128)
    te.wait_ge(sCst.sem, const_n1)
    ve.wait_ge(sCst.sem, const_n1)
    sc.wait_ge(sCst.sem, const_n1)
    sync.wait_ge(sCst.sem, const_n1)
    if stop_after == "s0":
        gp.wait_ge(sG.sem, sG.n)
        sync.wait_ge(sLd.sem, sLd.n)
        st.close()
        return nc

    def emit_ln_tile(xt_ap, xo_ap):
        vsync(ve.reduce_sum(out=statv[:, 0:1], in_=xt_ap, axis=AX.X, negate=True))
        inc(sc, sSc, sc.activation(out=sqs, in_=xt_ap, func=AF.Square,
                                   accum_out=statv[:, 1:2]))
        vsync(ve.tensor_scalar(out=statv[:, 2:3], in0=statv[:, 0:1],
                               scalar1=float(1.0 / D), scalar2=None, op0=ALU.mult))
        ve.wait_ge(sSc.sem, sSc.n)
        vsync(ve.tensor_tensor(out=statv[:, 3:4], in0=statv[:, 2:3],
                               in1=statv[:, 2:3], op=ALU.mult))
        inc(ve, sCv, ve.scalar_tensor_tensor(
            out=statv[:, 4:5], in0=statv[:, 1:2], scalar=float(1.0 / D),
            in1=statv[:, 3:4], op0=ALU.mult, op1=ALU.subtract))
        sc.wait_ge(sCv.sem, sCv.n)
        inc(sc, sSc, sc.activation(out=statv[:, 5:6], in_=statv[:, 4:5],
                                   func=AF.Sqrt, bias=f32(EPS)))
        ve.wait_ge(sSc.sem, sSc.n)
        vsync(ve.reciprocal(out=statv[:, 6:7], in_=statv[:, 5:6]))
        inc(ve, sCv, ve.tensor_scalar(out=xo_ap, in0=xt_ap, scalar1=statv[:, 2:3],
                                      scalar2=statv[:, 6:7], op0=ALU.add,
                                      op1=ALU.mult))

    ln_ready = {}
    tr_done = {}
    s2a_cv = {}
    slab_stored = {}
    for slab_i in range(8):
        for t4 in range(4):
            ti = slab_i * 4 + t4
            buf = ti % 2
            if ti >= 2:
                sync.wait_ge(sCv.sem, ln_ready[ti - 2])
            dma(sync, sLd, xtile[:, buf * D:(buf + 1) * D],
                x_kv[ti * 128:(ti + 1) * 128, :])
            ve.wait_ge(sLd.sem, sLd.n)
            sc.wait_ge(sLd.sem, sLd.n)
            if ti >= 2:
                ve.wait_ge(sMM.sem, tr_done[ti - 2])
            if slab_i > 0 and t4 == 0:
                ve.wait_ge(sG.sem, slab_stored[slab_i - 1])
            emit_ln_tile(xtile[:, buf * D:(buf + 1) * D],
                         xln[:, buf * D:(buf + 1) * D])
            ln_ready[ti] = sCv.n
            if stop_after == "tile0" and ti == 0:
                gp.wait_ge(sG.sem, sG.n)
                sync.wait_ge(sLd.sem, sLd.n)
                ve.wait_ge(sCv.sem, sCv.n)
                sc.wait_ge(sSc.sem, sSc.n)
                st.close()
                return nc
            te.wait_ge(sCv.sem, ln_ready[ti])
            if slab_i > 0 and t4 == 0:
                te.wait_ge(sG.sem, slab_stored[slab_i - 1])
            for q4 in range(8):
                qb = (ti * 8 + q4) % 2
                if stop_after != "t0a" and (ti > 0 or q4 > 0):
                    te.wait_ge(sEv.sem, sEv.n)
                for k4 in range(4):
                    ch = q4 * 4 + k4
                    _li = te.transpose(psT[:, k4 * 128:(k4 + 1) * 128],
                                 xln[:, buf * D + ch * 128:buf * D + (ch + 1) * 128],
                                 iden_sb[:])
                inc(te, sMM, _li)
                if stop_after == "t0a":
                    continue
                sc.wait_ge(sMM.sem, sMM.n)
                if ti * 8 + q4 >= 2:
                    sc.wait_ge(sVe.sem, s2a_cv[ti * 8 + q4 - 2])
                for k4 in range(4):
                    _li = sc.copy(out=evst[:, qb * 512 + k4 * 128:qb * 512 + (k4 + 1) * 128],
                            in_=psT[:, k4 * 128:(k4 + 1) * 128])
                inc(sc, sEv, _li)
                if stop_after == "t0d":
                    s2a_cv[ti * 8 + q4] = sVe.n
                    continue
                ve.wait_ge(sEv.sem, sEv.n)
                for k4 in range(4):
                    ch = q4 * 4 + k4
                    _li = ve.tensor_copy(
                        out=slabT[:, ch * 512 + t4 * 128:ch * 512 + (t4 + 1) * 128],
                        in_=evst[:, qb * 512 + k4 * 128:qb * 512 + (k4 + 1) * 128])
                inc(ve, sVe, _li)
                s2a_cv[ti * 8 + q4] = sVe.n
            tr_done[ti] = sMM.n
            if stop_after in ("t0a", "t0b", "t0d") and ti == 0:
                gp.wait_ge(sG.sem, sG.n)
                sync.wait_ge(sLd.sem, sLd.n)
                ve.wait_ge(sCv.sem, sCv.n)
                sc.wait_ge(sSc.sem, sSc.n)
                te.wait_ge(sMM.sem, sMM.n)
                st.close()
                return nc
        gp.wait_ge(sVe.sem, sVe.n)
        for ch in range(NCH):
            dma(gp, sG, xlt_d[ch, :, slab_i * 512:(slab_i + 1) * 512],
                slabT[:, ch * 512:(ch + 1) * 512])
        slab_stored[slab_i] = sG.n
        if stop_after == f"slab{slab_i}":
            gp.wait_ge(sG.sem, sG.n)
            sync.wait_ge(sLd.sem, sLd.n)
            ve.wait_ge(sCv.sem, sCv.n)
            sc.wait_ge(sSc.sem, sSc.n)
            te.wait_ge(sMM.sem, sMM.n)
            st.close()
            return nc
    s2a_done = sG.n
    if stop_after == "s2a":
        gp.wait_ge(sG.sem, sG.n)
        gp.dma_start(out=y.rearrange("a b -> (a b)").rearrange("(p f) -> p f", p=128),
                     in_=xlt_d4[0].ap().rearrange("a p t -> (a p t)").rearrange("(p f) -> p f", p=128)).then_inc(sSt.sem, 16)
        sSt.n += 16
        gp.wait_ge(sSt.sem, sSt.n)
        sync.wait_ge(sLd.sem, sLd.n)
        te.wait_ge(sMM.sem, sMM.n) if sMM.n else None
        sc.wait_ge(sSc.sem, sSc.n) if sSc.n else None
        ve.wait_ge(sVe.sem, sVe.n) if sVe.n else None
        st.close()
        return nc

    deq = {"idx": 0}
    deq_done = {}

    def emit_dequant(qw_ap, sc_ap, n_cols, dst_view, dst_off_cols, chunk_stride):
        for ch in range(NCH):
            i = deq["idx"]
            ib, sb2 = i % 4, i % 2
            if i >= 4:
                sync.wait_ge(sCv.sem, deq_done[i - 4])
            dma(sync, sLd, wint[:, ib * 512:ib * 512 + n_cols],
                qw_ap[ch * 128:(ch + 1) * 128, :])
            if i >= 2:
                gp.wait_ge(sCv.sem, deq_done[i - 2])
            dma(gp, sG, scb[:, sb2 * 1024:sb2 * 1024 + n_cols],
                sc_ap[ch:ch + 1, :].broadcast_to([128, n_cols]))
            ve.wait_ge(sLd.sem, sLd.n)
            ve.wait_ge(sG.sem, sG.n)
            o = dst_off_cols + ch * chunk_stride
            inc(ve, sCv, ve.scalar_tensor_tensor(
                out=dst_view[:, o:o + n_cols],
                in0=wint[:, ib * 512:ib * 512 + n_cols], scalar=-8.0,
                in1=scb[:, sb2 * 1024:sb2 * 1024 + n_cols],
                op0=ALU.add, op1=ALU.mult))
            deq_done[i] = sCv.n
            deq["idx"] += 1

    sync.wait_ge(sG.sem, s2a_done)
    ve.wait_ge(sG.sem, s2a_done)
    for sl in range(2):
        emit_dequant(qw_k[:, sl * 512:(sl + 1) * 512], sc_k[:, sl * 512:(sl + 1) * 512],
                     512, wkv, sl * 512, 1024)
    wk_ready = sCv.n

    kv_mm = {}
    kv_evac = {}
    kv_store = {}
    unit = 0
    PS6 = [psA[:, 0:512], psA[:, 512:1024], psA[:, 1024:1536],
           psB[:, 0:512], psB[:, 512:1024], psB[:, 1024:1536]]
    for slab_i in range(8):
        if slab_i >= 1:
            sync.wait_ge(sMM.sem, kv_mm[("slabuser", slab_i - 1)])
        for ch in range(NCH):
            dma(sync, sLd, slab[:, ch * 512:(ch + 1) * 512],
                xlt_d[ch, :, slab_i * 512:(slab_i + 1) * 512])
        te.wait_ge(sLd.sem, sLd.n)
        if slab_i == 0:
            te.wait_ge(sCv.sem, wk_ready)
        for g in range(NKV):
            ps = PS6[unit % 6]
            if unit >= 6:
                te.wait_ge(sVe.sem, kv_evac[unit - 6])
            for ch in range(NCH):
                _li = te.matmul(ps, wkv[:, ch * 1024 + g * 128:ch * 1024 + (g + 1) * 128],
                          slab[:, ch * 512:(ch + 1) * 512],
                          start=(ch == 0), stop=(ch == NCH - 1))
            inc(te, sMM, _li)
            kv_mm[unit] = sMM.n
            evs = unit % 4
            sc.wait_ge(sMM.sem, kv_mm[unit])
            if unit >= 1:
                sc.wait_ge(sVe.sem, kv_evac[unit - 1])
            if unit >= 4:
                sc.wait_ge(sG.sem, kv_store[unit - 4])
            sc.activation(out=rt2[0:64, :], in_=ps[64:128, :], func=AF.Copy,
                          scale=f32(-1.0))
            inc(sc, sSc, sc.activation(out=rt2[64:128, :], in_=ps[0:64, :],
                                       func=AF.Copy))
            ve.wait_ge(sSc.sem, sSc.n)
            ve.tensor_tensor(out=rt1[:], in0=ps,
                             in1=cosb[:, slab_i * 512:(slab_i + 1) * 512], op=ALU.mult)
            ve.tensor_tensor(out=rt2[:], in0=rt2[:],
                             in1=sinb[:, slab_i * 512:(slab_i + 1) * 512], op=ALU.mult)
            inc(ve, sVe, ve.tensor_tensor(out=ev[:, evs * 512:(evs + 1) * 512],
                                          in0=rt1[:], in1=rt2[:], op=ALU.add))
            kv_evac[unit] = sVe.n
            gp.wait_ge(sVe.sem, kv_evac[unit])
            dma(gp, sG, k_d[g, :, slab_i * 512:(slab_i + 1) * 512],
                ev[:, evs * 512:(evs + 1) * 512])
            kv_store[unit] = sG.n
            unit += 1
        kv_mm[("slabuser", slab_i)] = sMM.n
    kpass_mm = sMM.n
    if stop_after == "kpass":
        gp.wait_ge(sG.sem, sG.n)
        gp.dma_start(out=y.rearrange("a b -> (a b)").rearrange("(p f) -> p f", p=128),
                     in_=k_d.ap().rearrange('a p t -> (a p t)').rearrange('(p f) -> p f', p=128)).then_inc(sSt.sem, 16)
        sSt.n += 16
        gp.wait_ge(sSt.sem, sSt.n)
        sync.wait_ge(sLd.sem, sLd.n)
        te.wait_ge(sMM.sem, sMM.n) if sMM.n else None
        sc.wait_ge(sSc.sem, sSc.n) if sSc.n else None
        ve.wait_ge(sVe.sem, sVe.n) if sVe.n else None
        st.close()
        return nc

    ve.wait_ge(sMM.sem, kpass_mm)
    for sl in range(2):
        emit_dequant(qw_v[:, sl * 512:(sl + 1) * 512], sc_v[:, sl * 512:(sl + 1) * 512],
                     512, wkv, sl * 512, 1024)
    wv_ready = sCv.n
    sc.wait_ge(sG.sem, kv_store[63])
    vunit = 0
    v_mm = {}
    v_evac = {}
    v_store = {}
    for slab_i in range(8):
        if slab_i >= 1:
            sync.wait_ge(sMM.sem, v_mm[("slabuser", slab_i - 1)])
        for ch in range(NCH):
            dma(sync, sLd, slab[:, ch * 512:(ch + 1) * 512],
                xlt_d[ch, :, slab_i * 512:(slab_i + 1) * 512])
        te.wait_ge(sLd.sem, sLd.n)
        if slab_i == 0:
            te.wait_ge(sCv.sem, wv_ready)
        for tc in range(4):
            for nsl in range(2):
                ps = PS6[vunit % 6]
                if vunit >= 6:
                    te.wait_ge(sEv.sem, v_evac[vunit - 6])
                for ch in range(NCH):
                    _li = te.matmul(ps, slab[:, ch * 512 + tc * 128:ch * 512 + (tc + 1) * 128],
                              wkv[:, ch * 1024 + nsl * 512:ch * 1024 + (nsl + 1) * 512],
                              start=(ch == 0), stop=(ch == NCH - 1))
                inc(te, sMM, _li)
                v_mm[vunit] = sMM.n
                evs = vunit % 4
                sc.wait_ge(sMM.sem, v_mm[vunit])
                if vunit >= 4:
                    sc.wait_ge(sG.sem, v_store[vunit - 4])
                inc(sc, sEv, sc.copy(out=ev[:, evs * 512:(evs + 1) * 512], in_=ps))
                v_evac[vunit] = sEv.n
                gp.wait_ge(sEv.sem, v_evac[vunit])
                tci = slab_i * 4 + tc
                for gg in range(4):
                    g = nsl * 4 + gg
                    dma(gp, sG, v_d[g, :, tci, :],
                        ev[:, evs * 512 + gg * 128:evs * 512 + (gg + 1) * 128])
                v_store[vunit] = sG.n
                vunit += 1
        v_mm[("slabuser", slab_i)] = sMM.n
    vpass_mm = sMM.n
    vpass_store = sG.n
    if stop_after == "vpass":
        gp.wait_ge(sG.sem, sG.n)
        gp.dma_start(out=y.rearrange("a b -> (a b)").rearrange("(p f) -> p f", p=128),
                     in_=v_d.ap().rearrange('a p c t -> (a p c t)').rearrange('(p f) -> p f', p=128)).then_inc(sSt.sem, 16)
        sSt.n += 16
        gp.wait_ge(sSt.sem, sSt.n)
        sync.wait_ge(sLd.sem, sLd.n)
        te.wait_ge(sMM.sem, sMM.n) if sMM.n else None
        sc.wait_ge(sSc.sem, sSc.n) if sSc.n else None
        ve.wait_ge(sVe.sem, sVe.n) if sVe.n else None
        st.close()
        return nc

    gp.wait_ge(sMM.sem, vpass_mm)
    for s in range(NSLOT):
        for c4 in range(4):
            gsrc = xlt_d4[c4].ap().rearrange("c p t -> p c t")[:, :, 0:128]
            gsrc = dataclasses.replace(gsrc.copy(),
                                       offset=gsrc.offset + s * 512 + offv)
            gdst = xqlt[:, :].rearrange("p (c sw) -> p c sw", sw=ROWS)[
                :, c4 * 8:(c4 + 1) * 8, s * 128:(s + 1) * 128]
            dma(gp, sG, gdst, gsrc)
    xq_gathered = sG.n
    if stop_after == "gather":
        gp.wait_ge(sG.sem, sG.n)
        sync.wait_ge(sLd.sem, sLd.n)
        te.wait_ge(sMM.sem, sMM.n) if sMM.n else None
        sc.wait_ge(sSc.sem, sSc.n) if sSc.n else None
        ve.wait_ge(sVe.sem, sVe.n) if sVe.n else None
        st.close()
        return nc

    sc.wait_ge(sG.sem, v_store[63])
    sc.wait_ge(sVe.sem, kv_evac[63])
    qunit = 0
    q_mm = {}
    q_evac = {}
    q_store = {}
    wq_ready = {}
    for ns in range(16):
        half = ns % 2
        dst = half * NCH * 256
        if ns >= 2:
            sync.wait_ge(sMM.sem, q_mm[("nsuser", ns - 2)])
            ve.wait_ge(sMM.sem, q_mm[("nsuser", ns - 2)])
        emit_dequant(qw_q[:, ns * 256:(ns + 1) * 256],
                     sc_q[:, ns * 256:(ns + 1) * 256], 256, wq2, dst, 256)
        wq_ready[ns] = sCv.n
        if ns == 0:
            te.wait_ge(sG.sem, xq_gathered)
        te.wait_ge(sCv.sem, wq_ready[ns])
        for nsub in range(2):
            for tt in range(2):
                ps = PS6[qunit % 6]
                if qunit >= 6:
                    te.wait_ge(sVe.sem, q_evac[qunit - 6])
                for ch in range(NCH):
                    _li = te.matmul(ps, wq2[:, dst + ch * 256 + nsub * 128:
                                      dst + ch * 256 + (nsub + 1) * 128],
                              xqlt[:, ch * ROWS + tt * 512:ch * ROWS + (tt + 1) * 512],
                              start=(ch == 0), stop=(ch == NCH - 1))
                inc(te, sMM, _li)
                q_mm[qunit] = sMM.n
                h = ns * 2 + nsub
                evs = qunit % 4
                sc.wait_ge(sMM.sem, q_mm[qunit])
                if qunit >= 1:
                    sc.wait_ge(sVe.sem, q_evac[qunit - 1])
                if qunit >= 4:
                    sc.wait_ge(sG.sem, q_store[qunit - 4])
                sc.activation(out=rt2[0:64, :], in_=ps[64:128, :], func=AF.Copy,
                              scale=f32(-1.0))
                inc(sc, sSc, sc.activation(out=rt2[64:128, :], in_=ps[0:64, :],
                                           func=AF.Copy))
                ve.wait_ge(sSc.sem, sSc.n)
                ve.tensor_tensor(out=rt1[:], in0=ps,
                                 in1=cqb[:, tt * 512:(tt + 1) * 512], op=ALU.mult)
                ve.tensor_tensor(out=rt2[:], in0=rt2[:],
                                 in1=sqb[:, tt * 512:(tt + 1) * 512], op=ALU.mult)
                inc(ve, sVe, ve.tensor_tensor(out=ev[:, evs * 512:(evs + 1) * 512],
                                              in0=rt1[:], in1=rt2[:], op=ALU.add))
                q_evac[qunit] = sVe.n
                gp.wait_ge(sVe.sem, q_evac[qunit])
                dma(gp, sG, q_d[h, :, tt * 512:(tt + 1) * 512],
                    ev[:, evs * 512:(evs + 1) * 512])
                q_store[qunit] = sG.n
                qunit += 1
        q_mm[("nsuser", ns)] = sMM.n
    qpass_store = sG.n
    qpass_mm = sMM.n
    if stop_after == "qpass":
        gp.wait_ge(sG.sem, sG.n)
        gp.dma_start(out=y.rearrange("a b -> (a b)").rearrange("(p f) -> p f", p=128),
                     in_=q_d.ap().rearrange('a p t -> (a p t)').rearrange('(p f) -> p f', p=128)).then_inc(sSt.sem, 16)
        sSt.n += 16
        gp.wait_ge(sSt.sem, sSt.n)
        sync.wait_ge(sLd.sem, sLd.n)
        te.wait_ge(sMM.sem, sMM.n) if sMM.n else None
        sc.wait_ge(sSc.sem, sSc.n) if sSc.n else None
        ve.wait_ge(sVe.sem, sVe.n) if sVe.n else None
        st.close()
        return nc

    kld = {}
    qld = {}
    g_done = {}
    aunit = 0
    a_ev = {}
    a_ve = {}
    a_pv = {}
    slot_i = 0
    sync.wait_ge(sG.sem, qpass_store)
    for g in range(NKV):
        gb = g % 2
        if g >= 2:
            sync.wait_ge(sMM.sem, g_done[g - 2])
        dma(sync, sLd, kgb[:, gb * S:(gb + 1) * S], k_d[g, :, :])
        dma(sync, sLd, vgb[:, gb * S:(gb + 1) * S],
            v_d[g, :, :, :].rearrange("p a n -> p (a n)"))
        kld[g] = sLd.n
        te.wait_ge(sLd.sem, kld[g])
        for s in range(NSLOT):
            ext = 4 * (s + 1)
            ngrp = (ext + 2) // 3
            qb2 = slot_i % 2
            if slot_i >= 2:
                sync.wait_ge(sMM.sem, a_pv.get(("slotlast", slot_i - 2), 0))
            dma(sync, sLd,
                qrhs[:, qb2 * 512:(qb2 + 1) * 512].rearrange("p (h t) -> p h t", t=128),
                q_d[4 * g:4 * g + 4, :, s * 128:(s + 1) * 128]
                .rearrange("h p t -> p h t"))
            qld[slot_i] = sLd.n
            inc(ve, sVe, ve.memset(ssum[:], 0.0))
            zs = sVe.n
            te.wait_ge(sLd.sem, qld[slot_i])
            rhs = qrhs[:, qb2 * 512:(qb2 + 1) * 512]
            for grp in range(ngrp):
                nn = min(3, ext - grp * 3)
                pset = psA if aunit % 2 == 0 else psB
                if aunit >= 2:
                    te.wait_ge(sEv.sem, a_ev[aunit - 2])
                for p in range(nn):
                    kc = grp * 3 + p
                    _li = te.matmul(pset[:, p * 512:(p + 1) * 512],
                              kgb[:, gb * S + kc * 128:gb * S + (kc + 1) * 128],
                              rhs, start=True, stop=True)
                inc(te, sMM, _li)
                smm = sMM.n
                ring = aunit % 3
                sc.wait_ge(sMM.sem, smm)
                if aunit >= 3:
                    sc.wait_ge(sMM.sem, a_pv[aunit - 3])
                inc(sc, sEv, sc.activation(
                    out=pstr[:, ring * 1536:ring * 1536 + nn * 512],
                    in_=pset[:, 0:nn * 512], func=AF.Exp,
                    scale=f32(ISQ), bias=f32(-SHIFT)))
                a_ev[aunit] = sEv.n
                ve.wait_ge(sEv.sem, a_ev[aunit])
                ve.wait_ge(sVe.sem, zs)
                for p in range(nn):
                    kc = grp * 3 + p
                    if kc >= ext - 4:
                        mi = kc - (ext - 4)
                        ve.tensor_tensor(
                            out=pstr[:, ring * 1536 + p * 512:ring * 1536 + (p + 1) * 512],
                            in0=pstr[:, ring * 1536 + p * 512:ring * 1536 + (p + 1) * 512],
                            in1=masks_sb[:, mi * 512:(mi + 1) * 512], op=ALU.mult)
                for p in range(nn):
                    _li = ve.tensor_tensor(out=ssum[:], in0=ssum[:],
                                     in1=pstr[:, ring * 1536 + p * 512:
                                              ring * 1536 + (p + 1) * 512],
                                     op=ALU.add)
                inc(ve, sVe, _li)
                a_ve[aunit] = sVe.n
                te.wait_ge(sVe.sem, a_ve[aunit])
                for p in range(nn):
                    kc = grp * 3 + p
                    _li = te.matmul(psC[:],
                              vgb[:, gb * S + kc * 128:gb * S + (kc + 1) * 128],
                              pstr[:, ring * 1536 + p * 512:ring * 1536 + (p + 1) * 512],
                              start=(kc == 0), stop=(kc == ext - 1))
                inc(te, sMM, _li)
                a_pv[aunit] = sMM.n
                aunit += 1
            te.wait_ge(sEv.sem, a_ev[aunit - 1])
            te.wait_ge(sVe.sem, a_ve[aunit - 1])
            for i in range(4):
                _li = te.transpose(psB[:, i * 128:(i + 1) * 128],
                             ssum[:, i * 128:(i + 1) * 128], iden32_sb[:])
            inc(te, sMM, _li)
            ve.wait_ge(sMM.sem, sMM.n)
            for i in range(4):
                _lv = ve.reduce_sum(out=linv[:, i:i + 1], in_=psB[:, i * 128:(i + 1) * 128],
                                    axis=AX.X)
            vsync(_lv)
            inc(ve, sVe, ve.reciprocal(out=linv[:, 4:8], in_=linv[:, 0:4]))
            te.wait_ge(sVe.sem, sVe.n)
            for i in range(4):
                _li = te.matmul(psB[0:1, 512 + i * 128:512 + (i + 1) * 128],
                          linv[:, 4 + i:5 + i], iden32_sb[:], start=True, stop=True)
            inc(te, sMM, _li)
            sc.wait_ge(sMM.sem, sMM.n)
            inc(sc, sSc, sc.copy(out=lrow[:], in_=psB[0:1, 512:1024]))
            te.wait_ge(sSc.sem, sSc.n)
            _li = te.matmul(psB[:, 0:512], ones32_sb[:], lrow[:], start=True, stop=True)
            inc(te, sMM, _li)
            sc.wait_ge(sMM.sem, sMM.n)
            inc(sc, sSc, sc.copy(out=linvb[:], in_=psB[:, 0:512]))
            scready = sSc.n
            ve.wait_ge(sSc.sem, scready)
            ve.wait_ge(sMM.sem, a_pv[aunit - 1])
            ctx_dst = ctx[:, :].rearrange("p (h t) -> p h t", t=ROWS)[
                :, 4 * g:4 * g + 4, s * 128:(s + 1) * 128]
            inc(ve, sVe, ve.tensor_tensor(
                out=ctx_dst, in0=psC[:].rearrange("p (h t) -> p h t", t=128),
                in1=linvb[:].rearrange("p (h t) -> p h t", t=128), op=ALU.mult))
            te.wait_ge(sVe.sem, sVe.n)
            a_pv[("slotlast", slot_i)] = sMM.n
            slot_i += 1
        g_done[g] = sMM.n
    attn_ve = sVe.n
    if stop_after == "attn":
        gp.wait_ge(sG.sem, sG.n)
        sync.wait_ge(sLd.sem, sLd.n)
        te.wait_ge(sMM.sem, sMM.n) if sMM.n else None
        sc.wait_ge(sSc.sem, sSc.n) if sSc.n else None
        ve.wait_ge(sVe.sem, sVe.n) if sVe.n else None
        st.close()
        return nc

    o_mm = {}
    o_ve = {}
    o_st = {}
    ounit = 0
    wo_ready = {}
    ve.wait_ge(sMM.sem, qpass_mm)
    ve.wait_ge(sG.sem, qpass_store)
    for ns in range(16):
        half = ns % 2
        dst = half * NCH * 256
        if ns >= 2:
            sync.wait_ge(sMM.sem, o_mm[("nsuser", ns - 2)])
            ve.wait_ge(sMM.sem, o_mm[("nsuser", ns - 2)])
        emit_dequant(qw_o[:, ns * 256:(ns + 1) * 256],
                     sc_o[:, ns * 256:(ns + 1) * 256], 256, wo2, dst, 256)
        wo_ready[ns] = sCv.n
        te.wait_ge(sCv.sem, wo_ready[ns])
        if ns == 0:
            te.wait_ge(sVe.sem, attn_ve)
        for tt in range(8):
            ps = PS6[ounit % 6]
            if ounit >= 6:
                te.wait_ge(sVe.sem, o_ve[ounit - 6])
            for ch in range(NCH):
                _li = te.matmul(ps[:, 0:256],
                          ctx[:, ch * ROWS + tt * 128:ch * ROWS + (tt + 1) * 128],
                          wo2[:, dst + ch * 256:dst + (ch + 1) * 256],
                          start=(ch == 0), stop=(ch == NCH - 1))
            inc(te, sMM, _li)
            o_mm[ounit] = sMM.n
            xb = ounit % 2
            if ounit >= 2:
                sync.wait_ge(sVe.sem, o_ve[ounit - 2])
            dma(sync, sLd, xqst[:, xb * 512:xb * 512 + 256],
                x_q[tt * 128:(tt + 1) * 128, ns * 256:(ns + 1) * 256])
            ve.wait_ge(sLd.sem, sLd.n)
            ve.wait_ge(sMM.sem, o_mm[ounit])
            evs = ounit % 4
            if ounit >= 4:
                ve.wait_ge(sG.sem, o_st[ounit - 4])
            inc(ve, sVe, ve.tensor_tensor(out=ev[:, evs * 512:evs * 512 + 256],
                                          in0=ps[:, 0:256],
                                          in1=xqst[:, xb * 512:xb * 512 + 256],
                                          op=ALU.add))
            o_ve[ounit] = sVe.n
            gp.wait_ge(sVe.sem, o_ve[ounit])
            dma(gp, sG, x1_d[tt * 128:(tt + 1) * 128, ns * 256:(ns + 1) * 256],
                ev[:, evs * 512:evs * 512 + 256])
            o_st[ounit] = sG.n
            ounit += 1
        o_mm[("nsuser", ns)] = sMM.n
    opass_mm = sMM.n
    x1_stored = sG.n
    if stop_after == "opass":
        gp.wait_ge(sG.sem, sG.n)
        gp.dma_start(out=y.rearrange("a b -> (a b)").rearrange("(p f) -> p f", p=128),
                     in_=x1_d.ap().rearrange('a b -> (a b)').rearrange('(p f) -> p f', p=128)).then_inc(sSt.sem, 16)
        sSt.n += 16
        gp.wait_ge(sSt.sem, sSt.n)
        sync.wait_ge(sLd.sem, sLd.n)
        te.wait_ge(sMM.sem, sMM.n) if sMM.n else None
        sc.wait_ge(sSc.sem, sSc.n) if sSc.n else None
        ve.wait_ge(sVe.sem, sVe.n) if sVe.n else None
        st.close()
        return nc

    sync.wait_ge(sG.sem, x1_stored)
    ve.wait_ge(sMM.sem, opass_mm)
    sc.wait_ge(sMM.sem, opass_mm)
    ln2_ready = {}
    tr2_done = {}
    s6_cv = {}
    for tt in range(8):
        buf = tt % 2
        if tt >= 2:
            sync.wait_ge(sCv.sem, ln2_ready[tt - 2])
            sync.wait_ge(sMM.sem, tr2_done[tt - 2])
        dma(sync, sLd, xln6[:, buf * D:(buf + 1) * D],
            x1_d[tt * 128:(tt + 1) * 128, :])
        ve.wait_ge(sLd.sem, sLd.n)
        inc(ve, sCv, ve.tensor_copy(out=xtile6[:, buf * D:(buf + 1) * D],
                                    in_=xln6[:, buf * D:(buf + 1) * D]))
        sc.wait_ge(sCv.sem, sCv.n)
        emit_ln_tile(xtile6[:, buf * D:(buf + 1) * D],
                     xln6[:, buf * D:(buf + 1) * D])
        ln2_ready[tt] = sCv.n
        te.wait_ge(sCv.sem, ln2_ready[tt])
        for q4 in range(8):
            qb = (tt * 8 + q4) % 2
            te.wait_ge(sEv.sem, sEv.n)
            for k4 in range(4):
                ch = q4 * 4 + k4
                _li = te.transpose(psT[:, k4 * 128:(k4 + 1) * 128],
                             xln6[:, buf * D + ch * 128:buf * D + (ch + 1) * 128],
                             iden_sb[:])
            inc(te, sMM, _li)
            sc.wait_ge(sMM.sem, sMM.n)
            if tt * 8 + q4 >= 2:
                sc.wait_ge(sVe.sem, s6_cv[tt * 8 + q4 - 2])
            for k4 in range(4):
                _li = sc.copy(out=evst[:, qb * 512 + k4 * 128:qb * 512 + (k4 + 1) * 128],
                        in_=psT[:, k4 * 128:(k4 + 1) * 128])
            inc(sc, sEv, _li)
            ve.wait_ge(sEv.sem, sEv.n)
            for k4 in range(4):
                ch = q4 * 4 + k4
                _li = ve.tensor_copy(
                    out=x1lt[:, ch * ROWS + tt * 128:ch * ROWS + (tt + 1) * 128],
                    in_=evst[:, qb * 512 + k4 * 128:qb * 512 + (k4 + 1) * 128])
            inc(ve, sVe, _li)
            s6_cv[tt * 8 + q4] = sVe.n
        tr2_done[tt] = sMM.n
    ln2_done = sVe.n
    if stop_after == "ln2":
        gp.wait_ge(sG.sem, sG.n)
        sync.wait_ge(sLd.sem, sLd.n)
        te.wait_ge(sMM.sem, sMM.n) if sMM.n else None
        sc.wait_ge(sSc.sem, sSc.n) if sSc.n else None
        ve.wait_ge(sVe.sem, sVe.n) if sVe.n else None
        st.close()
        return nc

    gp.wait_ge(sVe.sem, ln2_done)
    dma(gp, sCst, aT_sb, aT)
    dma(gp, sCst, bstk_sb, bstack)
    dma(gp, sCst, bb_sb[0:1, 0:D], bb)
    late_ld = sCst.n

    sc.wait_ge(sVe.sem, ln2_done)
    te.wait_ge(sVe.sem, ln2_done)
    for tt in range(8):
        for ch in range(NCH):
            _li = te.matmul(psC[:, 0:8],
                      x1lt[:, ch * ROWS + tt * 128:ch * ROWS + (tt + 1) * 128],
                      wrT_sb[:, ch * 8:(ch + 1) * 8],
                      start=(ch == 0), stop=(ch == NCH - 1))
        inc(te, sMM, _li)
        sc.wait_ge(sMM.sem, sMM.n)
        inc(sc, sSc, sc.copy(out=lgs[:, tt * 8:(tt + 1) * 8], in_=psC[:, 0:8]))
        te.wait_ge(sSc.sem, sSc.n)
    lg_ready = sSc.n
    ve.wait_ge(sSc.sem, lg_ready)
    tps = E(nc.sbuf_tensor("tps", [128, 64], F32))
    wfb_mm = 0
    for tt in range(8):
        L = lgs[:, tt * 8:(tt + 1) * 8]
        m1n = statv[:, 0:1]
        t0 = statv[:, 1:2]
        e2c = statv[:, 2:3]
        w1 = statv[:, 3:4]
        w2 = statv[:, 4:5]
        vsync(ve.reduce_max(out=m1n, in_=L, axis=AX.X, negate=True))
        vsync(ve.tensor_scalar(out=tps[:, 0:8], in0=L, scalar1=m1n, scalar2=None,
                               op0=ALU.add))
        vsync(ve.tensor_single_scalar(out=tps[:, 8:16], in_=tps[:, 0:8],
                                      scalar=0.0, op=ALU.is_ge))
        vsync(ve.scalar_tensor_tensor(out=tps[:, 16:24], in0=tps[:, 8:16],
                                      scalar=-1e30, in1=L, op0=ALU.mult, op1=ALU.add))
        vsync(ve.reduce_max(out=t0, in_=tps[:, 16:24], axis=AX.X, negate=True))
        vsync(ve.tensor_scalar(out=tps[:, 24:32], in0=tps[:, 16:24], scalar1=t0,
                               scalar2=None, op0=ALU.add))
        vsync(ve.tensor_single_scalar(out=tps[:, 32:40], in_=tps[:, 24:32],
                                      scalar=0.0, op=ALU.is_ge))
        inc(ve, sCv, ve.tensor_tensor(out=e2c, in0=m1n, in1=t0, op=ALU.subtract))
        sc.wait_ge(sCv.sem, sCv.n)
        inc(sc, sSc, sc.activation(out=e2c, in_=e2c, func=AF.Exp))
        ve.wait_ge(sSc.sem, sSc.n)
        vsync(ve.tensor_scalar(out=t0, in0=e2c, scalar1=1.0, scalar2=None,
                               op0=ALU.add))
        vsync(ve.reciprocal(out=w1, in_=t0))
        vsync(ve.tensor_tensor(out=w2, in0=e2c, in1=w1, op=ALU.mult))
        vsync(ve.tensor_scalar(out=tps[:, 40:48], in0=tps[:, 8:16], scalar1=w1,
                               scalar2=None, op0=ALU.mult))
        vsync(ve.tensor_scalar(out=tps[:, 48:56], in0=tps[:, 32:40], scalar1=w2,
                               scalar2=None, op0=ALU.mult))
        vsync(ve.tensor_tensor(out=tps[:, 56:64], in0=tps[:, 40:48], in1=tps[:, 48:56],
                               op=ALU.add))
        if tt > 0:
            ve.wait_ge(sMM.sem, wfb_mm)
        inc(ve, sCv, ve.tensor_copy(out=wfb[:], in_=tps[:, 56:64]))
        te.wait_ge(sCv.sem, sCv.n)
        if tt > 0:
            te.wait_ge(sSc.sem, sSc.n)
        _li = te.transpose(psT[0:8, 0:128], wfb[:], iden_sb[:])
        inc(te, sMM, _li)
        wfb_mm = sMM.n
        sc.wait_ge(sMM.sem, sMM.n)
        inc(sc, sSc, sc.copy(out=wtr[0:8, tt * 128:(tt + 1) * 128],
                             in_=psT[0:8, 0:128]))
    wtr_ready = sSc.n
    te.wait_ge(sSc.sem, wtr_ready)
    for halfq in range(2):
        _li = te.matmul(psA[:, halfq * 512:(halfq + 1) * 512], e2_sb[:],
                  wtr[:, halfq * 512:(halfq + 1) * 512], start=True, stop=True)
    inc(te, sMM, _li)
    sc.wait_ge(sMM.sem, sMM.n)
    inc(sc, sSc, sc.copy(out=wbc[:], in_=psA[:, 0:1024]))
    wbc_ready = sSc.n
    te.wait_ge(sSc.sem, wbc_ready)
    te.wait_ge(sCst.sem, late_ld)
    for halfq in range(2):
        for ch in range(NCH):
            _li = te.matmul(psB[:, halfq * 512:(halfq + 1) * 512],
                      aT_sb[:, ch * 128:(ch + 1) * 128],
                      x1lt[:, ch * ROWS + halfq * 512:ch * ROWS + (halfq + 1) * 512],
                      start=(ch == 0), stop=(ch == NCH - 1))
    inc(te, sMM, _li)
    ve.wait_ge(sMM.sem, sMM.n)
    ve.wait_ge(sSc.sem, wbc_ready)
    inc(ve, sVe, ve.tensor_tensor(out=hw[:], in0=psB[:, 0:1024], in1=wbc[:],
                                  op=ALU.mult))
    hw_ready = sVe.n
    if stop_after == "s7":
        gp.wait_ge(sG.sem, sG.n)
        sync.wait_ge(sLd.sem, sLd.n)
        te.wait_ge(sMM.sem, sMM.n) if sMM.n else None
        sc.wait_ge(sSc.sem, sSc.n) if sSc.n else None
        ve.wait_ge(sVe.sem, sVe.n) if sVe.n else None
        st.close()
        return nc

    te.wait_ge(sVe.sem, hw_ready)
    gp.wait_ge(sMM.sem, sMM.n)
    funit = 0
    f_mm = {}
    f_ve = {}
    f_st = {}
    wb_ld = {}
    for ns in range(16):
        half = ns % 2
        dst = half * NCH * 256
        if ns >= 2:
            gp.wait_ge(sMM.sem, f_mm[("nsuser", ns - 2)])
        dma(gp, sG, wb2[:, dst:dst + NCH * 256],
            wbT[:, ns * NCH * 256:(ns + 1) * NCH * 256])
        wb_ld[ns] = sG.n
        te.wait_ge(sG.sem, wb_ld[ns])
        for tt in range(8):
            ps = PS6[funit % 6]
            if funit >= 6:
                te.wait_ge(sVe.sem, f_ve[funit - 6])
            for ch in range(NCH):
                te.matmul(ps[:, 0:256],
                          x1lt[:, ch * ROWS + tt * 128:ch * ROWS + (tt + 1) * 128],
                          wb2[:, dst + ch * 256:dst + (ch + 1) * 256],
                          start=(ch == 0), stop=False)
            te.matmul(ps[:, 0:256], ones_sb[:], bb_sb[0:1, ns * 256:(ns + 1) * 256],
                      start=False, stop=False)
            _li = te.matmul(ps[:, 0:256], hw[:, tt * 128:(tt + 1) * 128],
                      bstk_sb[:, ns * 256:(ns + 1) * 256], start=False, stop=True)
            inc(te, sMM, _li)
            f_mm[funit] = sMM.n
            xb = funit % 2
            if funit >= 2:
                sync.wait_ge(sVe.sem, f_ve[funit - 2])
            dma(sync, sLd, x1st[:, xb * 512:xb * 512 + 256],
                x1_d[tt * 128:(tt + 1) * 128, ns * 256:(ns + 1) * 256])
            ve.wait_ge(sLd.sem, sLd.n)
            ve.wait_ge(sMM.sem, f_mm[funit])
            ob = funit % 2
            if funit >= 2:
                ve.wait_ge(sSt.sem, f_st[funit - 2])
            inc(ve, sVe, ve.tensor_tensor(out=outst[:, ob * 512:ob * 512 + 256],
                                          in0=ps[:, 0:256],
                                          in1=x1st[:, xb * 512:xb * 512 + 256],
                                          op=ALU.add))
            f_ve[funit] = sVe.n
            gp.wait_ge(sVe.sem, f_ve[funit])
            gp.dma_start(out=y[tt * 128:(tt + 1) * 128, ns * 256:(ns + 1) * 256],
                         in_=outst[:, ob * 512:ob * 512 + 256]).then_inc(sSt.sem, 16)
            sSt.n += 16
            f_st[funit] = sSt.n
            funit += 1
        f_mm[("nsuser", ns)] = sMM.n

    gp.wait_ge(sSt.sem, sSt.n)
    gp.wait_ge(sG.sem, sG.n)
    sync.wait_ge(sLd.sem, sLd.n)

    st.close()
    return nc


def _host_inputs(inputs):
    x = np.asarray(inputs["L0_attn_hidden_states"])
    qq = np.ascontiguousarray(np.asarray(inputs["L0_attn_qw_q"]))
    qk = np.ascontiguousarray(np.asarray(inputs["L0_attn_qw_k"]))
    qv = np.ascontiguousarray(np.asarray(inputs["L0_attn_qw_v"]))
    qo = np.ascontiguousarray(np.asarray(inputs["L0_attn_qw_o"]))
    scq = np.ascontiguousarray(np.asarray(inputs["L0_attn_scales_q"]))
    sck = np.ascontiguousarray(np.asarray(inputs["L0_attn_scales_k"]))
    scv = np.ascontiguousarray(np.asarray(inputs["L0_attn_scales_v"]))
    sco = np.ascontiguousarray(np.asarray(inputs["L0_attn_scales_o"]))
    wb = np.asarray(inputs["L0_ffn_W_base"])
    bbv = np.ascontiguousarray(np.asarray(inputs["L0_ffn_b_base"]).reshape(1, D))
    wr = np.asarray(inputs["L0_ffn_W_router"])
    A = np.asarray(inputs["L0_ffn_A"])
    B = np.asarray(inputs["L0_ffn_B"])

    wbT0 = wb.T
    wbT = np.ascontiguousarray(
        wbT0.reshape(32, 128, 16, 256).transpose(1, 2, 0, 3).reshape(128, 16 * 32 * 256))
    wrT = np.ascontiguousarray(
        wr.T.reshape(32, 128, 8).transpose(1, 0, 2).reshape(128, 256))
    aT = np.ascontiguousarray(
        A.reshape(128, D).T.reshape(32, 128, 128).transpose(1, 0, 2).reshape(128, D))
    bstack = np.ascontiguousarray(B.transpose(0, 2, 1).reshape(128, D))

    inv = 1.0 / (10000.0 ** (np.arange(0, HD, 2, dtype=np.float64) / HD))
    invf = np.concatenate([inv, inv])
    pos = np.arange(S, dtype=np.float64)
    ang = invf[:, None] * pos[None, :]
    cos_kv = np.cos(ang).astype(np.float32)
    sin_kv = np.sin(ang).astype(np.float32)
    iden_b = np.eye(128, dtype=ml_dtypes.bfloat16)
    iden32 = np.eye(128, dtype=np.float32)
    e2mv = (np.kron(np.eye(8), np.ones((1, 16))) * 2.0).astype(np.float32)
    ones_r = np.ones((1, 128), dtype=ml_dtypes.bfloat16)
    ones32_r = np.ones((1, 128), dtype=np.float32)
    tri = np.triu(np.ones((128, 128), dtype=np.float32))

    shared = dict(qw_q=qq, qw_k=qk, qw_v=qv, qw_o=qo, sc_q=scq, sc_k=sck,
                  sc_v=scv, sc_o=sco, wbT=wbT, bb=bbv, wrT=wrT, aT=aT,
                  bstack=bstack, cos_kv=cos_kv, sin_kv=sin_kv,
                  iden=iden_b, iden32=iden32, e2m=e2mv, ones_r=ones_r,
                  ones32_r=ones32_r)
    in_maps = []
    for c in range(8):
        b, cp = c // 4, c % 4
        rows = np.concatenate([np.arange((4 * s + cp) * 128, (4 * s + cp) * 128 + 128)
                               for s in range(NSLOT)])
        x_qv = np.ascontiguousarray(x[b][rows, :])
        x_kvv = np.ascontiguousarray(x[b])
        cos_qv = np.ascontiguousarray(cos_kv[:, rows])
        sin_qv = np.ascontiguousarray(sin_kv[:, rows])
        mk = np.zeros((4, 128, 512), dtype=np.float32)
        for p in range(4):
            if p < cp:
                mk[p] = 1.0
            elif p == cp:
                mk[p] = np.tile(tri, (1, 4))
        in_maps.append(dict(shared, x_kv=x_kvv, x_q=x_qv, cos_q=cos_qv,
                            sin_q=sin_qv, masks=mk.astype(ml_dtypes.bfloat16),
                            mypid=np.full((1, 1), c, np.uint32)))
    return in_maps


def kernel(**inputs):
    in_maps = _host_inputs(inputs)
    nc = build_nc()
    res = run_bass_kernel_spmd(nc, in_maps, core_ids=list(range(8)))
    out = np.empty((2, S, D), dtype=np.float32)
    for c in range(8):
        b, cp = c // 4, c % 4
        yc = np.asarray(res.results[c]["y"])
        for s in range(NSLOT):
            out[b, (4 * s + cp) * 128:(4 * s + cp) * 128 + 128, :] = \
                yc[s * 128:(s + 1) * 128, :]
    return out


# version 06 (final)
# speedup vs baseline: 1.1647x; 1.1647x over previous
import dataclasses
import math
from contextlib import ExitStack

import numpy as np
import ml_dtypes

import concourse.bass as bass
import concourse.mybir as mybir
from concourse.bass_utils import run_bass_kernel_spmd
from concourse.expressions_rust import RuntimeValue

F32 = mybir.dt.float32
BF16 = mybir.dt.bfloat16
I32 = mybir.dt.int32
U32 = mybir.dt.uint32
AF = mybir.ActivationFunctionType
ALU = mybir.AluOpType
AX = mybir.AxisListType

D = 4096
S = 4096
NH = 32
NKV = 8
HD = 128
ROWS = 1024
NSLOT = 8
NCH = 32
SHIFT = 20.0
ISQ = 1.0 / math.sqrt(HD)
EPS = 1e-5
f32 = np.float32
KB = 1024


class Ctr:
    def __init__(self, sem):
        self.sem = sem
        self.n = 0


def build_nc(stop_after=None):
    nc = bass.Bass("TRN2", target_bir_lowering=False, debug=False)

    def din(name, shape, dt=F32):
        return nc.dram_tensor(name, shape, dt, kind="ExternalInput").ap()

    x_kv = din("x_kv", [S, D])
    x_q = din("x_q", [ROWS, D])
    qw_q = din("qw_q", [D, NH * HD], I32)
    qw_k = din("qw_k", [D, NKV * HD], I32)
    qw_v = din("qw_v", [D, NKV * HD], I32)
    qw_o = din("qw_o", [NH * HD, D], I32)
    sc_q = din("sc_q", [NCH, NH * HD])
    sc_k = din("sc_k", [NCH, NKV * HD])
    sc_v = din("sc_v", [NCH, NKV * HD])
    sc_o = din("sc_o", [NCH, D])
    wbT = din("wbT", [128, 16 * NCH * 256])
    bb = din("bb", [1, D])
    wrT = din("wrT", [128, NCH * 8])
    aT = din("aT", [128, NCH * 128])
    bstack = din("bstack", [128, D])
    cos_kv = din("cos_kv", [HD, S])
    sin_kv = din("sin_kv", [HD, S])
    cos_q = din("cos_q", [HD, ROWS])
    sin_q = din("sin_q", [HD, ROWS])
    masks = din("masks", [4, 128, 512], BF16)
    iden = din("iden", [128, 128], BF16)
    iden32 = din("iden32", [128, 128])
    e2m = din("e2m", [8, 128])
    ones_r = din("ones_r", [1, 128], BF16)
    ones32_r = din("ones32_r", [1, 128])
    mypid = din("mypid", [1, 1], U32)

    y = nc.dram_tensor("y", [ROWS, D], F32, kind="ExternalOutput").ap()

    xlt_d4 = [nc.dram_tensor(f"xlt_d{i}", [8, 128, S], BF16, kind="Internal")
              for i in range(4)]

    class _XltView:
        def __getitem__(self, idx):
            ch = idx[0] if isinstance(idx, tuple) else idx
            rest = idx[1:] if isinstance(idx, tuple) else (slice(None), slice(None))
            return xlt_d4[ch // 8][(ch % 8,) + tuple(rest)]
    xlt_d = _XltView()
    q_d = nc.dram_tensor("q_d", [NH, HD, ROWS], BF16, kind="Internal")
    k_d = nc.dram_tensor("k_d", [NKV, HD, S], BF16, kind="Internal")
    v_d = nc.dram_tensor("v_d", [NKV, 128, NCH, HD], BF16, kind="Internal")
    x1_d = nc.dram_tensor("x1_d", [ROWS, D], BF16, kind="Internal")

    st = ExitStack()
    E = st.enter_context

    iden_sb = E(nc.sbuf_tensor("iden_sb", [128, 128], BF16))
    iden32_sb = E(nc.sbuf_tensor("iden32_sb", [128, 128], F32))
    ones_sb = E(nc.sbuf_tensor("ones_sb", [1, 128], BF16))
    ones32_sb = E(nc.sbuf_tensor("ones32_sb", [1, 128], F32))
    e2_sb = E(nc.sbuf_tensor("e2_sb", [8, 128], F32))
    masks_sb = E(nc.sbuf_tensor("masks_sb", [128, 4 * 512], BF16))
    wrT_sb = E(nc.sbuf_tensor("wrT_sb", [128, NCH * 8], BF16))
    pid_sb = E(nc.sbuf_tensor("pid_sb", [1, 1], U32))
    statv = E(nc.sbuf_tensor("statv", [128, 16], F32))
    wint = E(nc.sbuf_tensor("wint", [128, 4 * 512], I32))
    scb = E(nc.sbuf_tensor("scb", [128, 2 * 1024], F32))
    ev = E(nc.sbuf_tensor("ev", [128, 4 * 512], BF16))
    rt1 = E(nc.sbuf_tensor("rt1", [128, 512], F32))
    rt2 = E(nc.sbuf_tensor("rt2", [128, 512], F32))
    ssum = E(nc.sbuf_tensor("ssum", [128, 512], F32))
    lrow = E(nc.sbuf_tensor("lrow", [1, 512], F32))
    linvb = E(nc.sbuf_tensor("linvb", [128, 512], F32))
    linv = E(nc.sbuf_tensor("linv", [128, 8], F32))
    xqst = E(nc.sbuf_tensor("xqst", [128, 2 * 512], F32))
    hw = E(nc.sbuf_tensor("hw", [128, ROWS], BF16))
    wbc = E(nc.sbuf_tensor("wbc", [128, ROWS], BF16))
    lgs = E(nc.sbuf_tensor("lgs", [128, 64], F32))
    wfb = E(nc.sbuf_tensor("wfb", [128, 8], BF16))
    wtr = E(nc.sbuf_tensor("wtr", [8, ROWS], F32))
    outst = E(nc.sbuf_tensor("outst", [128, 2 * 512], F32))
    x1st = E(nc.sbuf_tensor("x1st", [128, 2 * 512], BF16))
    evst = E(nc.sbuf_tensor("evst", [128, 2 * 512], F32))

    ARN = E(nc.sbuf_tensor("ARN", [128, 136 * KB], mybir.dt.uint8))

    def av(off_kb, size_kb, dt):
        return ARN[:, off_kb * KB:(off_kb + size_kb) * KB].bitcast(dt)

    xtile = av(0, 32, F32)
    xln = av(32, 16, BF16)
    sqs = av(48, 16, F32)
    slabT = av(64, 32, BF16)
    cosb = av(96, 16, F32)
    sinb = av(112, 16, F32)
    cqb = av(128, 4, F32)
    sqb = av(132, 4, F32)
    wkv = av(0, 64, BF16)
    slab = av(64, 32, BF16)
    xqlt = av(0, 64, BF16)
    wq2 = av(64, 32, BF16)
    ctx = av(0, 64, BF16)
    pstr = av(64, 9, BF16)
    qrhs = av(73, 2, BF16)
    kgb = av(96, 16, BF16)
    vgb = av(112, 16, BF16)
    wo2 = av(64, 32, BF16)
    x1lt = av(64, 64, BF16)
    xtile6 = av(0, 32, F32)
    xln6 = av(32, 16, BF16)
    sqs6 = av(48, 16, F32)
    wb2 = av(0, 32, BF16)
    aT_sb = av(32, 8, BF16)
    bstk_sb = av(40, 8, BF16)
    bb_sb = av(48, 8, BF16)

    psA = E(nc.psum_tensor("psA", [128, 1536], F32))
    psB = E(nc.psum_tensor("psB", [128, 1536], F32))
    psC = E(nc.psum_tensor("psC", [128, 512], F32))
    psT = E(nc.psum_tensor("psT", [128, 512], BF16))

    sLd = Ctr(E(nc.semaphore("sLd")))
    sCst = Ctr(E(nc.semaphore("sCst")))
    sWt = [Ctr(E(nc.semaphore("sWt0"))), Ctr(E(nc.semaphore("sWt1")))]
    sSb = [Ctr(E(nc.semaphore("sSb0"))), Ctr(E(nc.semaphore("sSb1")))]
    sSl = Ctr(E(nc.semaphore("sSl")))
    sKgC = Ctr(E(nc.semaphore("sKgC")))
    sQrC = Ctr(E(nc.semaphore("sQrC")))
    sXqC = Ctr(E(nc.semaphore("sXqC")))
    sX1C = Ctr(E(nc.semaphore("sX1C")))
    sX6C = Ctr(E(nc.semaphore("sX6C")))
    sG = Ctr(E(nc.semaphore("sG")))
    sCv = Ctr(E(nc.semaphore("sCv")))
    sSc = Ctr(E(nc.semaphore("sSc")))
    sMM = Ctr(E(nc.semaphore("sMM")))
    sEv = Ctr(E(nc.semaphore("sEv")))
    sVe = Ctr(E(nc.semaphore("sVe")))
    sSt = Ctr(E(nc.semaphore("sSt")))

    sync, gp, te, sc, ve = nc.sync, nc.gpsimd, nc.tensor, nc.scalar, nc.vector

    def dma(eng, ctr, out, in_):
        eng.wait_ge(ctr.sem, ctr.n)
        eng.dma_start(out=out, in_=in_).then_inc(ctr.sem, 16)
        ctr.n += 16

    def inc(eng, ctr, inst=None):
        (inst if inst is not None else eng.nop()).then_inc(ctr.sem, 1)
        ctr.n += 1

    def vsync(inst):
        inst.then_inc(sCv.sem, 1)
        sCv.n += 1
        ve.wait_ge(sCv.sem, sCv.n)

    dma(sync, sCst, iden_sb[:], iden)
    dma(sync, sCst, iden32_sb[:], iden32)
    dma(sync, sCst, ones_sb[:], ones_r)
    dma(sync, sCst, ones32_sb[:], ones32_r)
    dma(sync, sCst, e2_sb[:], e2m)
    dma(sync, sCst, masks_sb[:, :].rearrange("p (m n) -> p m n", n=512),
        masks.rearrange("m p n -> p m n"))
    dma(sync, sCst, cosb, cos_kv)
    dma(sync, sCst, sinb, sin_kv)
    dma(sync, sCst, cqb, cos_q)
    dma(sync, sCst, sqb, sin_q)
    dma(sync, sCst, pid_sb[:], mypid)
    dma(gp, sCst, wrT_sb[:], wrT)
    const_n1 = sCst.n

    gp.wait_ge(sCst.sem, const_n1)
    preg = gp.alloc_register("preg")
    gp.load(preg, pid_sb[0:1, 0:1])
    cpreg = gp.alloc_register("cpreg")
    gp.reg_alu(cpreg, bass.RegisterHandles(preg), 4, ALU.mod)
    offreg = gp.alloc_register("offreg")
    gp.reg_alu(offreg, bass.RegisterHandles(cpreg), 128, ALU.mult)
    offv = gp.snap(RuntimeValue(offreg), min_val=0, max_val=384,
                   guaranteed_mod_val=128)
    te.wait_ge(sCst.sem, const_n1)
    ve.wait_ge(sCst.sem, const_n1)
    sc.wait_ge(sCst.sem, const_n1)
    sync.wait_ge(sCst.sem, const_n1)
    if stop_after == "s0":
        gp.wait_ge(sG.sem, sG.n)
        sync.wait_ge(sLd.sem, sLd.n)
        st.close()
        return nc

    def emit_ln_tile(xt_ap, xo_ap):
        vsync(ve.reduce_sum(out=statv[:, 0:1], in_=xt_ap, axis=AX.X, negate=True))
        inc(sc, sSc, sc.activation(out=sqs, in_=xt_ap, func=AF.Square,
                                   accum_out=statv[:, 1:2]))
        vsync(ve.tensor_scalar(out=statv[:, 2:3], in0=statv[:, 0:1],
                               scalar1=float(1.0 / D), scalar2=None, op0=ALU.mult))
        ve.wait_ge(sSc.sem, sSc.n)
        vsync(ve.tensor_tensor(out=statv[:, 3:4], in0=statv[:, 2:3],
                               in1=statv[:, 2:3], op=ALU.mult))
        inc(ve, sCv, ve.scalar_tensor_tensor(
            out=statv[:, 4:5], in0=statv[:, 1:2], scalar=float(1.0 / D),
            in1=statv[:, 3:4], op0=ALU.mult, op1=ALU.subtract))
        sc.wait_ge(sCv.sem, sCv.n)
        inc(sc, sSc, sc.activation(out=statv[:, 5:6], in_=statv[:, 4:5],
                                   func=AF.Sqrt, bias=f32(EPS)))
        ve.wait_ge(sSc.sem, sSc.n)
        vsync(ve.reciprocal(out=statv[:, 6:7], in_=statv[:, 5:6]))
        inc(ve, sCv, ve.tensor_scalar(out=xo_ap, in0=xt_ap, scalar1=statv[:, 2:3],
                                      scalar2=statv[:, 6:7], op0=ALU.add,
                                      op1=ALU.mult))

    ln_ready = {}
    tr_done = {}
    s2a_cv = {}
    slab_stored = {}
    for slab_i in range(8):
        for t4 in range(4):
            ti = slab_i * 4 + t4
            buf = ti % 2
            if ti >= 2:
                sync.wait_ge(sCv.sem, ln_ready[ti - 2])
            dma(sync, sLd, xtile[:, buf * D:(buf + 1) * D],
                x_kv[ti * 128:(ti + 1) * 128, :])
            ve.wait_ge(sLd.sem, sLd.n)
            sc.wait_ge(sLd.sem, sLd.n)
            if ti >= 2:
                ve.wait_ge(sMM.sem, tr_done[ti - 2])
            if slab_i > 0 and t4 == 0:
                ve.wait_ge(sG.sem, slab_stored[slab_i - 1])
            emit_ln_tile(xtile[:, buf * D:(buf + 1) * D],
                         xln[:, buf * D:(buf + 1) * D])
            ln_ready[ti] = sCv.n
            if stop_after == "tile0" and ti == 0:
                gp.wait_ge(sG.sem, sG.n)
                sync.wait_ge(sLd.sem, sLd.n)
                ve.wait_ge(sCv.sem, sCv.n)
                sc.wait_ge(sSc.sem, sSc.n)
                st.close()
                return nc
            te.wait_ge(sCv.sem, ln_ready[ti])
            if slab_i > 0 and t4 == 0:
                te.wait_ge(sG.sem, slab_stored[slab_i - 1])
            for q4 in range(8):
                qb = (ti * 8 + q4) % 2
                if stop_after != "t0a" and (ti > 0 or q4 > 0):
                    te.wait_ge(sEv.sem, sEv.n)
                for k4 in range(4):
                    ch = q4 * 4 + k4
                    _li = te.transpose(psT[:, k4 * 128:(k4 + 1) * 128],
                                 xln[:, buf * D + ch * 128:buf * D + (ch + 1) * 128],
                                 iden_sb[:])
                inc(te, sMM, _li)
                if stop_after == "t0a":
                    continue
                sc.wait_ge(sMM.sem, sMM.n)
                if ti * 8 + q4 >= 2:
                    sc.wait_ge(sVe.sem, s2a_cv[ti * 8 + q4 - 2])
                for k4 in range(4):
                    _li = sc.copy(out=evst[:, qb * 512 + k4 * 128:qb * 512 + (k4 + 1) * 128],
                            in_=psT[:, k4 * 128:(k4 + 1) * 128])
                inc(sc, sEv, _li)
                if stop_after == "t0d":
                    s2a_cv[ti * 8 + q4] = sVe.n
                    continue
                ve.wait_ge(sEv.sem, sEv.n)
                for k4 in range(4):
                    ch = q4 * 4 + k4
                    _li = ve.tensor_copy(
                        out=slabT[:, ch * 512 + t4 * 128:ch * 512 + (t4 + 1) * 128],
                        in_=evst[:, qb * 512 + k4 * 128:qb * 512 + (k4 + 1) * 128])
                inc(ve, sVe, _li)
                s2a_cv[ti * 8 + q4] = sVe.n
            tr_done[ti] = sMM.n
            if stop_after in ("t0a", "t0b", "t0d") and ti == 0:
                gp.wait_ge(sG.sem, sG.n)
                sync.wait_ge(sLd.sem, sLd.n)
                ve.wait_ge(sCv.sem, sCv.n)
                sc.wait_ge(sSc.sem, sSc.n)
                te.wait_ge(sMM.sem, sMM.n)
                st.close()
                return nc
        gp.wait_ge(sVe.sem, sVe.n)
        for ch in range(NCH):
            dma(gp, sG, xlt_d[ch, :, slab_i * 512:(slab_i + 1) * 512],
                slabT[:, ch * 512:(ch + 1) * 512])
        slab_stored[slab_i] = sG.n
        if stop_after == f"slab{slab_i}":
            gp.wait_ge(sG.sem, sG.n)
            sync.wait_ge(sLd.sem, sLd.n)
            ve.wait_ge(sCv.sem, sCv.n)
            sc.wait_ge(sSc.sem, sSc.n)
            te.wait_ge(sMM.sem, sMM.n)
            st.close()
            return nc
    s2a_done = sG.n
    if stop_after == "s2a":
        gp.wait_ge(sG.sem, sG.n)
        gp.dma_start(out=y.rearrange("a b -> (a b)").rearrange("(p f) -> p f", p=128),
                     in_=xlt_d4[0].ap().rearrange("a p t -> (a p t)").rearrange("(p f) -> p f", p=128)).then_inc(sSt.sem, 16)
        sSt.n += 16
        gp.wait_ge(sSt.sem, sSt.n)
        sync.wait_ge(sLd.sem, sLd.n)
        te.wait_ge(sMM.sem, sMM.n) if sMM.n else None
        sc.wait_ge(sSc.sem, sSc.n) if sSc.n else None
        ve.wait_ge(sVe.sem, sVe.n) if sVe.n else None
        st.close()
        return nc

    deq = {"idx": 0}
    deq_done = {}

    def emit_dequant(qw_ap, sc_ap, n_cols, dst_view, dst_off_cols, chunk_stride):
        for ch in range(NCH):
            i = deq["idx"]
            ib, sb2 = i % 4, i % 2
            if i >= 4:
                sync.wait_ge(sCv.sem, deq_done[i - 4])
            dma(sync, sWt[i % 2], wint[:, ib * 512:ib * 512 + n_cols],
                qw_ap[ch * 128:(ch + 1) * 128, :])
            if i >= 2:
                gp.wait_ge(sCv.sem, deq_done[i - 2])
            dma(gp, sSb[i % 2], scb[:, sb2 * 1024:sb2 * 1024 + n_cols],
                sc_ap[ch:ch + 1, :].broadcast_to([128, n_cols]))
            ve.wait_ge(sWt[i % 2].sem, sWt[i % 2].n)
            ve.wait_ge(sSb[i % 2].sem, sSb[i % 2].n)
            o = dst_off_cols + ch * chunk_stride
            inc(ve, sCv, ve.scalar_tensor_tensor(
                out=dst_view[:, o:o + n_cols],
                in0=wint[:, ib * 512:ib * 512 + n_cols], scalar=-8.0,
                in1=scb[:, sb2 * 1024:sb2 * 1024 + n_cols],
                op0=ALU.add, op1=ALU.mult))
            deq_done[i] = sCv.n
            deq["idx"] += 1

    sync.wait_ge(sG.sem, s2a_done)
    ve.wait_ge(sG.sem, s2a_done)
    for sl in range(2):
        emit_dequant(qw_k[:, sl * 512:(sl + 1) * 512], sc_k[:, sl * 512:(sl + 1) * 512],
                     512, wkv, sl * 512, 1024)
    wk_ready = sCv.n

    kv_mm = {}
    kv_evac = {}
    kv_store = {}
    unit = 0
    PS6 = [psA[:, 0:512], psA[:, 512:1024], psA[:, 1024:1536],
           psB[:, 0:512], psB[:, 512:1024], psB[:, 1024:1536]]
    for slab_i in range(8):
        if slab_i >= 1:
            sync.wait_ge(sMM.sem, kv_mm[("slabuser", slab_i - 1)])
        for ch in range(NCH):
            dma(sync, sLd, slab[:, ch * 512:(ch + 1) * 512],
                xlt_d[ch, :, slab_i * 512:(slab_i + 1) * 512])
        te.wait_ge(sLd.sem, sLd.n)
        if slab_i == 0:
            te.wait_ge(sCv.sem, wk_ready)
        for g in range(NKV):
            ps = PS6[unit % 6]
            if unit >= 6:
                te.wait_ge(sVe.sem, kv_evac[unit - 6])
            for ch in range(NCH):
                _li = te.matmul(ps, wkv[:, ch * 1024 + g * 128:ch * 1024 + (g + 1) * 128],
                          slab[:, ch * 512:(ch + 1) * 512],
                          start=(ch == 0), stop=(ch == NCH - 1))
            inc(te, sMM, _li)
            kv_mm[unit] = sMM.n
            evs = unit % 4
            sc.wait_ge(sMM.sem, kv_mm[unit])
            if unit >= 1:
                sc.wait_ge(sVe.sem, kv_evac[unit - 1])
            if unit >= 4:
                sc.wait_ge(sG.sem, kv_store[unit - 4])
            sc.activation(out=rt2[0:64, :], in_=ps[64:128, :], func=AF.Copy,
                          scale=f32(-1.0))
            inc(sc, sSc, sc.activation(out=rt2[64:128, :], in_=ps[0:64, :],
                                       func=AF.Copy))
            ve.wait_ge(sSc.sem, sSc.n)
            ve.tensor_tensor(out=rt1[:], in0=ps,
                             in1=cosb[:, slab_i * 512:(slab_i + 1) * 512], op=ALU.mult)
            ve.tensor_tensor(out=rt2[:], in0=rt2[:],
                             in1=sinb[:, slab_i * 512:(slab_i + 1) * 512], op=ALU.mult)
            inc(ve, sVe, ve.tensor_tensor(out=ev[:, evs * 512:(evs + 1) * 512],
                                          in0=rt1[:], in1=rt2[:], op=ALU.add))
            kv_evac[unit] = sVe.n
            gp.wait_ge(sVe.sem, kv_evac[unit])
            dma(gp, sG, k_d[g, :, slab_i * 512:(slab_i + 1) * 512],
                ev[:, evs * 512:(evs + 1) * 512])
            kv_store[unit] = sG.n
            unit += 1
        kv_mm[("slabuser", slab_i)] = sMM.n
    kpass_mm = sMM.n
    if stop_after == "kpass":
        gp.wait_ge(sG.sem, sG.n)
        gp.dma_start(out=y.rearrange("a b -> (a b)").rearrange("(p f) -> p f", p=128),
                     in_=k_d.ap().rearrange('a p t -> (a p t)').rearrange('(p f) -> p f', p=128)).then_inc(sSt.sem, 16)
        sSt.n += 16
        gp.wait_ge(sSt.sem, sSt.n)
        sync.wait_ge(sLd.sem, sLd.n)
        te.wait_ge(sMM.sem, sMM.n) if sMM.n else None
        sc.wait_ge(sSc.sem, sSc.n) if sSc.n else None
        ve.wait_ge(sVe.sem, sVe.n) if sVe.n else None
        st.close()
        return nc

    ve.wait_ge(sMM.sem, kpass_mm)
    for sl in range(2):
        emit_dequant(qw_v[:, sl * 512:(sl + 1) * 512], sc_v[:, sl * 512:(sl + 1) * 512],
                     512, wkv, sl * 512, 1024)
    wv_ready = sCv.n
    sc.wait_ge(sG.sem, kv_store[63])
    vunit = 0
    v_mm = {}
    v_evac = {}
    v_store = {}
    for slab_i in range(8):
        if slab_i >= 1:
            sync.wait_ge(sMM.sem, v_mm[("slabuser", slab_i - 1)])
        for ch in range(NCH):
            dma(sync, sSl, slab[:, ch * 512:(ch + 1) * 512],
                xlt_d[ch, :, slab_i * 512:(slab_i + 1) * 512])
        te.wait_ge(sSl.sem, sSl.n)
        if slab_i == 0:
            te.wait_ge(sCv.sem, wv_ready)
        for tc in range(4):
            for nsl in range(2):
                ps = PS6[vunit % 6]
                if vunit >= 6:
                    te.wait_ge(sEv.sem, v_evac[vunit - 6])
                for ch in range(NCH):
                    _li = te.matmul(ps, slab[:, ch * 512 + tc * 128:ch * 512 + (tc + 1) * 128],
                              wkv[:, ch * 1024 + nsl * 512:ch * 1024 + (nsl + 1) * 512],
                              start=(ch == 0), stop=(ch == NCH - 1))
                inc(te, sMM, _li)
                v_mm[vunit] = sMM.n
                evs = vunit % 4
                sc.wait_ge(sMM.sem, v_mm[vunit])
                if vunit >= 4:
                    sc.wait_ge(sG.sem, v_store[vunit - 4])
                inc(sc, sEv, sc.copy(out=ev[:, evs * 512:(evs + 1) * 512], in_=ps))
                v_evac[vunit] = sEv.n
                gp.wait_ge(sEv.sem, v_evac[vunit])
                tci = slab_i * 4 + tc
                for gg in range(4):
                    g = nsl * 4 + gg
                    dma(gp, sG, v_d[g, :, tci, :],
                        ev[:, evs * 512 + gg * 128:evs * 512 + (gg + 1) * 128])
                v_store[vunit] = sG.n
                vunit += 1
        v_mm[("slabuser", slab_i)] = sMM.n
    vpass_mm = sMM.n
    vpass_store = sG.n
    if stop_after == "vpass":
        gp.wait_ge(sG.sem, sG.n)
        gp.dma_start(out=y.rearrange("a b -> (a b)").rearrange("(p f) -> p f", p=128),
                     in_=v_d.ap().rearrange('a p c t -> (a p c t)').rearrange('(p f) -> p f', p=128)).then_inc(sSt.sem, 16)
        sSt.n += 16
        gp.wait_ge(sSt.sem, sSt.n)
        sync.wait_ge(sLd.sem, sLd.n)
        te.wait_ge(sMM.sem, sMM.n) if sMM.n else None
        sc.wait_ge(sSc.sem, sSc.n) if sSc.n else None
        ve.wait_ge(sVe.sem, sVe.n) if sVe.n else None
        st.close()
        return nc

    gp.wait_ge(sMM.sem, vpass_mm)
    for s in range(NSLOT):
        for c4 in range(4):
            gsrc = xlt_d4[c4].ap().rearrange("c p t -> p c t")[:, :, 0:128]
            gsrc = dataclasses.replace(gsrc.copy(),
                                       offset=gsrc.offset + s * 512 + offv)
            gdst = xqlt[:, :].rearrange("p (c sw) -> p c sw", sw=ROWS)[
                :, c4 * 8:(c4 + 1) * 8, s * 128:(s + 1) * 128]
            dma(gp, sG, gdst, gsrc)
    xq_gathered = sG.n
    if stop_after == "gather":
        gp.wait_ge(sG.sem, sG.n)
        sync.wait_ge(sLd.sem, sLd.n)
        te.wait_ge(sMM.sem, sMM.n) if sMM.n else None
        sc.wait_ge(sSc.sem, sSc.n) if sSc.n else None
        ve.wait_ge(sVe.sem, sVe.n) if sVe.n else None
        st.close()
        return nc

    sc.wait_ge(sG.sem, v_store[63])
    sc.wait_ge(sVe.sem, kv_evac[63])
    qunit = 0
    q_mm = {}
    q_evac = {}
    q_store = {}
    wq_ready = {}
    for ns in range(16):
        half = ns % 2
        dst = half * NCH * 256
        if ns >= 2:
            sync.wait_ge(sMM.sem, q_mm[("nsuser", ns - 2)])
            ve.wait_ge(sMM.sem, q_mm[("nsuser", ns - 2)])
        emit_dequant(qw_q[:, ns * 256:(ns + 1) * 256],
                     sc_q[:, ns * 256:(ns + 1) * 256], 256, wq2, dst, 256)
        wq_ready[ns] = sCv.n
        if ns == 0:
            te.wait_ge(sG.sem, xq_gathered)
        te.wait_ge(sCv.sem, wq_ready[ns])
        for nsub in range(2):
            for tt in range(2):
                ps = PS6[qunit % 6]
                if qunit >= 6:
                    te.wait_ge(sVe.sem, q_evac[qunit - 6])
                for ch in range(NCH):
                    _li = te.matmul(ps, wq2[:, dst + ch * 256 + nsub * 128:
                                      dst + ch * 256 + (nsub + 1) * 128],
                              xqlt[:, ch * ROWS + tt * 512:ch * ROWS + (tt + 1) * 512],
                              start=(ch == 0), stop=(ch == NCH - 1))
                inc(te, sMM, _li)
                q_mm[qunit] = sMM.n
                h = ns * 2 + nsub
                evs = qunit % 4
                sc.wait_ge(sMM.sem, q_mm[qunit])
                if qunit >= 1:
                    sc.wait_ge(sVe.sem, q_evac[qunit - 1])
                if qunit >= 4:
                    sc.wait_ge(sG.sem, q_store[qunit - 4])
                sc.activation(out=rt2[0:64, :], in_=ps[64:128, :], func=AF.Copy,
                              scale=f32(-1.0))
                inc(sc, sSc, sc.activation(out=rt2[64:128, :], in_=ps[0:64, :],
                                           func=AF.Copy))
                ve.wait_ge(sSc.sem, sSc.n)
                ve.tensor_tensor(out=rt1[:], in0=ps,
                                 in1=cqb[:, tt * 512:(tt + 1) * 512], op=ALU.mult)
                ve.tensor_tensor(out=rt2[:], in0=rt2[:],
                                 in1=sqb[:, tt * 512:(tt + 1) * 512], op=ALU.mult)
                inc(ve, sVe, ve.tensor_tensor(out=ev[:, evs * 512:(evs + 1) * 512],
                                              in0=rt1[:], in1=rt2[:], op=ALU.add))
                q_evac[qunit] = sVe.n
                gp.wait_ge(sVe.sem, q_evac[qunit])
                dma(gp, sG, q_d[h, :, tt * 512:(tt + 1) * 512],
                    ev[:, evs * 512:(evs + 1) * 512])
                q_store[qunit] = sG.n
                qunit += 1
        q_mm[("nsuser", ns)] = sMM.n
    qpass_store = sG.n
    qpass_mm = sMM.n
    if stop_after == "qpass":
        gp.wait_ge(sG.sem, sG.n)
        gp.dma_start(out=y.rearrange("a b -> (a b)").rearrange("(p f) -> p f", p=128),
                     in_=q_d.ap().rearrange('a p t -> (a p t)').rearrange('(p f) -> p f', p=128)).then_inc(sSt.sem, 16)
        sSt.n += 16
        gp.wait_ge(sSt.sem, sSt.n)
        sync.wait_ge(sLd.sem, sLd.n)
        te.wait_ge(sMM.sem, sMM.n) if sMM.n else None
        sc.wait_ge(sSc.sem, sSc.n) if sSc.n else None
        ve.wait_ge(sVe.sem, sVe.n) if sVe.n else None
        st.close()
        return nc

    kld = {}
    qld = {}
    g_done = {}
    aunit = 0
    a_ev = {}
    a_ve = {}
    a_pv = {}
    slot_i = 0
    sync.wait_ge(sG.sem, qpass_store)
    for g in range(NKV):
        gb = g % 2
        if g >= 2:
            sync.wait_ge(sMM.sem, g_done[g - 2])
        dma(sync, sKgC, kgb[:, gb * S:(gb + 1) * S], k_d[g, :, :])
        dma(sync, sKgC, vgb[:, gb * S:(gb + 1) * S],
            v_d[g, :, :, :].rearrange("p a n -> p (a n)"))
        kld[g] = sKgC.n
        te.wait_ge(sKgC.sem, kld[g])
        for s in range(NSLOT):
            ext = 4 * (s + 1)
            ngrp = (ext + 2) // 3
            qb2 = slot_i % 2
            if slot_i >= 2:
                sync.wait_ge(sMM.sem, a_pv.get(("slotlast", slot_i - 2), 0))
            dma(sync, sQrC,
                qrhs[:, qb2 * 512:(qb2 + 1) * 512].rearrange("p (h t) -> p h t", t=128),
                q_d[4 * g:4 * g + 4, :, s * 128:(s + 1) * 128]
                .rearrange("h p t -> p h t"))
            qld[slot_i] = sQrC.n
            inc(ve, sVe, ve.memset(ssum[:], 0.0))
            zs = sVe.n
            te.wait_ge(sQrC.sem, qld[slot_i])
            rhs = qrhs[:, qb2 * 512:(qb2 + 1) * 512]
            for grp in range(ngrp):
                nn = min(3, ext - grp * 3)
                pset = psA if aunit % 2 == 0 else psB
                if aunit >= 2:
                    te.wait_ge(sEv.sem, a_ev[aunit - 2])
                for p in range(nn):
                    kc = grp * 3 + p
                    _li = te.matmul(pset[:, p * 512:(p + 1) * 512],
                              kgb[:, gb * S + kc * 128:gb * S + (kc + 1) * 128],
                              rhs, start=True, stop=True)
                inc(te, sMM, _li)
                smm = sMM.n
                ring = aunit % 3
                sc.wait_ge(sMM.sem, smm)
                if aunit >= 3:
                    sc.wait_ge(sMM.sem, a_pv[aunit - 3])
                inc(sc, sEv, sc.activation(
                    out=pstr[:, ring * 1536:ring * 1536 + nn * 512],
                    in_=pset[:, 0:nn * 512], func=AF.Exp,
                    scale=f32(ISQ), bias=f32(-SHIFT)))
                a_ev[aunit] = sEv.n
                ve.wait_ge(sEv.sem, a_ev[aunit])
                ve.wait_ge(sVe.sem, zs)
                for p in range(nn):
                    kc = grp * 3 + p
                    if kc >= ext - 4:
                        mi = kc - (ext - 4)
                        ve.tensor_tensor(
                            out=pstr[:, ring * 1536 + p * 512:ring * 1536 + (p + 1) * 512],
                            in0=pstr[:, ring * 1536 + p * 512:ring * 1536 + (p + 1) * 512],
                            in1=masks_sb[:, mi * 512:(mi + 1) * 512], op=ALU.mult)
                for p in range(nn):
                    _li = ve.tensor_tensor(out=ssum[:], in0=ssum[:],
                                     in1=pstr[:, ring * 1536 + p * 512:
                                              ring * 1536 + (p + 1) * 512],
                                     op=ALU.add)
                inc(ve, sVe, _li)
                a_ve[aunit] = sVe.n
                te.wait_ge(sVe.sem, a_ve[aunit])
                for p in range(nn):
                    kc = grp * 3 + p
                    _li = te.matmul(psC[:],
                              vgb[:, gb * S + kc * 128:gb * S + (kc + 1) * 128],
                              pstr[:, ring * 1536 + p * 512:ring * 1536 + (p + 1) * 512],
                              start=(kc == 0), stop=(kc == ext - 1))
                inc(te, sMM, _li)
                a_pv[aunit] = sMM.n
                aunit += 1
            te.wait_ge(sEv.sem, a_ev[aunit - 1])
            te.wait_ge(sVe.sem, a_ve[aunit - 1])
            for i in range(4):
                _li = te.transpose(psB[:, i * 128:(i + 1) * 128],
                             ssum[:, i * 128:(i + 1) * 128], iden32_sb[:])
            inc(te, sMM, _li)
            ve.wait_ge(sMM.sem, sMM.n)
            for i in range(4):
                _lv = ve.reduce_sum(out=linv[:, i:i + 1], in_=psB[:, i * 128:(i + 1) * 128],
                                    axis=AX.X)
            vsync(_lv)
            inc(ve, sVe, ve.reciprocal(out=linv[:, 4:8], in_=linv[:, 0:4]))
            te.wait_ge(sVe.sem, sVe.n)
            for i in range(4):
                _li = te.matmul(psB[0:1, 512 + i * 128:512 + (i + 1) * 128],
                          linv[:, 4 + i:5 + i], iden32_sb[:], start=True, stop=True)
            inc(te, sMM, _li)
            sc.wait_ge(sMM.sem, sMM.n)
            inc(sc, sSc, sc.copy(out=lrow[:], in_=psB[0:1, 512:1024]))
            te.wait_ge(sSc.sem, sSc.n)
            _li = te.matmul(psB[:, 0:512], ones32_sb[:], lrow[:], start=True, stop=True)
            inc(te, sMM, _li)
            sc.wait_ge(sMM.sem, sMM.n)
            inc(sc, sSc, sc.copy(out=linvb[:], in_=psB[:, 0:512]))
            scready = sSc.n
            ve.wait_ge(sSc.sem, scready)
            ve.wait_ge(sMM.sem, a_pv[aunit - 1])
            ctx_dst = ctx[:, :].rearrange("p (h t) -> p h t", t=ROWS)[
                :, 4 * g:4 * g + 4, s * 128:(s + 1) * 128]
            inc(ve, sVe, ve.tensor_tensor(
                out=ctx_dst, in0=psC[:].rearrange("p (h t) -> p h t", t=128),
                in1=linvb[:].rearrange("p (h t) -> p h t", t=128), op=ALU.mult))
            te.wait_ge(sVe.sem, sVe.n)
            a_pv[("slotlast", slot_i)] = sMM.n
            slot_i += 1
        g_done[g] = sMM.n
    attn_ve = sVe.n
    if stop_after == "attn":
        gp.wait_ge(sG.sem, sG.n)
        sync.wait_ge(sLd.sem, sLd.n)
        te.wait_ge(sMM.sem, sMM.n) if sMM.n else None
        sc.wait_ge(sSc.sem, sSc.n) if sSc.n else None
        ve.wait_ge(sVe.sem, sVe.n) if sVe.n else None
        st.close()
        return nc

    o_mm = {}
    o_ve = {}
    o_st = {}
    ounit = 0
    wo_ready = {}
    ve.wait_ge(sMM.sem, qpass_mm)
    ve.wait_ge(sG.sem, qpass_store)
    for ns in range(16):
        half = ns % 2
        dst = half * NCH * 256
        if ns >= 2:
            sync.wait_ge(sMM.sem, o_mm[("nsuser", ns - 2)])
            ve.wait_ge(sMM.sem, o_mm[("nsuser", ns - 2)])
        emit_dequant(qw_o[:, ns * 256:(ns + 1) * 256],
                     sc_o[:, ns * 256:(ns + 1) * 256], 256, wo2, dst, 256)
        wo_ready[ns] = sCv.n
        te.wait_ge(sCv.sem, wo_ready[ns])
        if ns == 0:
            te.wait_ge(sVe.sem, attn_ve)
        for tt in range(8):
            ps = PS6[ounit % 6]
            if ounit >= 6:
                te.wait_ge(sVe.sem, o_ve[ounit - 6])
            for ch in range(NCH):
                _li = te.matmul(ps[:, 0:256],
                          ctx[:, ch * ROWS + tt * 128:ch * ROWS + (tt + 1) * 128],
                          wo2[:, dst + ch * 256:dst + (ch + 1) * 256],
                          start=(ch == 0), stop=(ch == NCH - 1))
            inc(te, sMM, _li)
            o_mm[ounit] = sMM.n
            xb = ounit % 2
            if ounit >= 2:
                sync.wait_ge(sVe.sem, o_ve[ounit - 2])
            dma(sync, sXqC, xqst[:, xb * 512:xb * 512 + 256],
                x_q[tt * 128:(tt + 1) * 128, ns * 256:(ns + 1) * 256])
            ve.wait_ge(sXqC.sem, sXqC.n)
            ve.wait_ge(sMM.sem, o_mm[ounit])
            evs = ounit % 4
            if ounit >= 4:
                ve.wait_ge(sG.sem, o_st[ounit - 4])
            inc(ve, sVe, ve.tensor_tensor(out=ev[:, evs * 512:evs * 512 + 256],
                                          in0=ps[:, 0:256],
                                          in1=xqst[:, xb * 512:xb * 512 + 256],
                                          op=ALU.add))
            o_ve[ounit] = sVe.n
            gp.wait_ge(sVe.sem, o_ve[ounit])
            dma(gp, sG, x1_d[tt * 128:(tt + 1) * 128, ns * 256:(ns + 1) * 256],
                ev[:, evs * 512:evs * 512 + 256])
            o_st[ounit] = sG.n
            ounit += 1
        o_mm[("nsuser", ns)] = sMM.n
    opass_mm = sMM.n
    x1_stored = sG.n
    if stop_after == "opass":
        gp.wait_ge(sG.sem, sG.n)
        gp.dma_start(out=y.rearrange("a b -> (a b)").rearrange("(p f) -> p f", p=128),
                     in_=x1_d.ap().rearrange('a b -> (a b)').rearrange('(p f) -> p f', p=128)).then_inc(sSt.sem, 16)
        sSt.n += 16
        gp.wait_ge(sSt.sem, sSt.n)
        sync.wait_ge(sLd.sem, sLd.n)
        te.wait_ge(sMM.sem, sMM.n) if sMM.n else None
        sc.wait_ge(sSc.sem, sSc.n) if sSc.n else None
        ve.wait_ge(sVe.sem, sVe.n) if sVe.n else None
        st.close()
        return nc

    sync.wait_ge(sG.sem, x1_stored)
    ve.wait_ge(sMM.sem, opass_mm)
    sc.wait_ge(sMM.sem, opass_mm)
    ln2_ready = {}
    tr2_done = {}
    s6_cv = {}
    for tt in range(8):
        buf = tt % 2
        if tt >= 2:
            sync.wait_ge(sCv.sem, ln2_ready[tt - 2])
            sync.wait_ge(sMM.sem, tr2_done[tt - 2])
        dma(sync, sX6C, xln6[:, buf * D:(buf + 1) * D],
            x1_d[tt * 128:(tt + 1) * 128, :])
        ve.wait_ge(sX6C.sem, sX6C.n)
        inc(ve, sCv, ve.tensor_copy(out=xtile6[:, buf * D:(buf + 1) * D],
                                    in_=xln6[:, buf * D:(buf + 1) * D]))
        sc.wait_ge(sCv.sem, sCv.n)
        emit_ln_tile(xtile6[:, buf * D:(buf + 1) * D],
                     xln6[:, buf * D:(buf + 1) * D])
        ln2_ready[tt] = sCv.n
        te.wait_ge(sCv.sem, ln2_ready[tt])
        for q4 in range(8):
            qb = (tt * 8 + q4) % 2
            te.wait_ge(sEv.sem, sEv.n)
            for k4 in range(4):
                ch = q4 * 4 + k4
                _li = te.transpose(psT[:, k4 * 128:(k4 + 1) * 128],
                             xln6[:, buf * D + ch * 128:buf * D + (ch + 1) * 128],
                             iden_sb[:])
            inc(te, sMM, _li)
            sc.wait_ge(sMM.sem, sMM.n)
            if tt * 8 + q4 >= 2:
                sc.wait_ge(sVe.sem, s6_cv[tt * 8 + q4 - 2])
            for k4 in range(4):
                _li = sc.copy(out=evst[:, qb * 512 + k4 * 128:qb * 512 + (k4 + 1) * 128],
                        in_=psT[:, k4 * 128:(k4 + 1) * 128])
            inc(sc, sEv, _li)
            ve.wait_ge(sEv.sem, sEv.n)
            for k4 in range(4):
                ch = q4 * 4 + k4
                _li = ve.tensor_copy(
                    out=x1lt[:, ch * ROWS + tt * 128:ch * ROWS + (tt + 1) * 128],
                    in_=evst[:, qb * 512 + k4 * 128:qb * 512 + (k4 + 1) * 128])
            inc(ve, sVe, _li)
            s6_cv[tt * 8 + q4] = sVe.n
        tr2_done[tt] = sMM.n
    ln2_done = sVe.n
    if stop_after == "ln2":
        gp.wait_ge(sG.sem, sG.n)
        sync.wait_ge(sLd.sem, sLd.n)
        te.wait_ge(sMM.sem, sMM.n) if sMM.n else None
        sc.wait_ge(sSc.sem, sSc.n) if sSc.n else None
        ve.wait_ge(sVe.sem, sVe.n) if sVe.n else None
        st.close()
        return nc

    gp.wait_ge(sVe.sem, ln2_done)
    dma(gp, sCst, aT_sb, aT)
    dma(gp, sCst, bstk_sb, bstack)
    dma(gp, sCst, bb_sb[0:1, 0:D], bb)
    late_ld = sCst.n

    sc.wait_ge(sVe.sem, ln2_done)
    te.wait_ge(sVe.sem, ln2_done)
    for tt in range(8):
        for ch in range(NCH):
            _li = te.matmul(psC[:, 0:8],
                      x1lt[:, ch * ROWS + tt * 128:ch * ROWS + (tt + 1) * 128],
                      wrT_sb[:, ch * 8:(ch + 1) * 8],
                      start=(ch == 0), stop=(ch == NCH - 1))
        inc(te, sMM, _li)
        sc.wait_ge(sMM.sem, sMM.n)
        inc(sc, sSc, sc.copy(out=lgs[:, tt * 8:(tt + 1) * 8], in_=psC[:, 0:8]))
        te.wait_ge(sSc.sem, sSc.n)
    lg_ready = sSc.n
    ve.wait_ge(sSc.sem, lg_ready)
    tps = E(nc.sbuf_tensor("tps", [128, 64], F32))
    wfb_mm = 0
    for tt in range(8):
        L = lgs[:, tt * 8:(tt + 1) * 8]
        m1n = statv[:, 0:1]
        t0 = statv[:, 1:2]
        e2c = statv[:, 2:3]
        w1 = statv[:, 3:4]
        w2 = statv[:, 4:5]
        vsync(ve.reduce_max(out=m1n, in_=L, axis=AX.X, negate=True))
        vsync(ve.tensor_scalar(out=tps[:, 0:8], in0=L, scalar1=m1n, scalar2=None,
                               op0=ALU.add))
        vsync(ve.tensor_single_scalar(out=tps[:, 8:16], in_=tps[:, 0:8],
                                      scalar=0.0, op=ALU.is_ge))
        vsync(ve.scalar_tensor_tensor(out=tps[:, 16:24], in0=tps[:, 8:16],
                                      scalar=-1e30, in1=L, op0=ALU.mult, op1=ALU.add))
        vsync(ve.reduce_max(out=t0, in_=tps[:, 16:24], axis=AX.X, negate=True))
        vsync(ve.tensor_scalar(out=tps[:, 24:32], in0=tps[:, 16:24], scalar1=t0,
                               scalar2=None, op0=ALU.add))
        vsync(ve.tensor_single_scalar(out=tps[:, 32:40], in_=tps[:, 24:32],
                                      scalar=0.0, op=ALU.is_ge))
        inc(ve, sCv, ve.tensor_tensor(out=e2c, in0=m1n, in1=t0, op=ALU.subtract))
        sc.wait_ge(sCv.sem, sCv.n)
        inc(sc, sSc, sc.activation(out=e2c, in_=e2c, func=AF.Exp))
        ve.wait_ge(sSc.sem, sSc.n)
        vsync(ve.tensor_scalar(out=t0, in0=e2c, scalar1=1.0, scalar2=None,
                               op0=ALU.add))
        vsync(ve.reciprocal(out=w1, in_=t0))
        vsync(ve.tensor_tensor(out=w2, in0=e2c, in1=w1, op=ALU.mult))
        vsync(ve.tensor_scalar(out=tps[:, 40:48], in0=tps[:, 8:16], scalar1=w1,
                               scalar2=None, op0=ALU.mult))
        vsync(ve.tensor_scalar(out=tps[:, 48:56], in0=tps[:, 32:40], scalar1=w2,
                               scalar2=None, op0=ALU.mult))
        vsync(ve.tensor_tensor(out=tps[:, 56:64], in0=tps[:, 40:48], in1=tps[:, 48:56],
                               op=ALU.add))
        if tt > 0:
            ve.wait_ge(sMM.sem, wfb_mm)
        inc(ve, sCv, ve.tensor_copy(out=wfb[:], in_=tps[:, 56:64]))
        te.wait_ge(sCv.sem, sCv.n)
        if tt > 0:
            te.wait_ge(sSc.sem, sSc.n)
        _li = te.transpose(psT[0:8, 0:128], wfb[:], iden_sb[:])
        inc(te, sMM, _li)
        wfb_mm = sMM.n
        sc.wait_ge(sMM.sem, sMM.n)
        inc(sc, sSc, sc.copy(out=wtr[0:8, tt * 128:(tt + 1) * 128],
                             in_=psT[0:8, 0:128]))
    wtr_ready = sSc.n
    te.wait_ge(sSc.sem, wtr_ready)
    for halfq in range(2):
        _li = te.matmul(psA[:, halfq * 512:(halfq + 1) * 512], e2_sb[:],
                  wtr[:, halfq * 512:(halfq + 1) * 512], start=True, stop=True)
    inc(te, sMM, _li)
    sc.wait_ge(sMM.sem, sMM.n)
    inc(sc, sSc, sc.copy(out=wbc[:], in_=psA[:, 0:1024]))
    wbc_ready = sSc.n
    te.wait_ge(sSc.sem, wbc_ready)
    te.wait_ge(sCst.sem, late_ld)
    for halfq in range(2):
        for ch in range(NCH):
            _li = te.matmul(psB[:, halfq * 512:(halfq + 1) * 512],
                      aT_sb[:, ch * 128:(ch + 1) * 128],
                      x1lt[:, ch * ROWS + halfq * 512:ch * ROWS + (halfq + 1) * 512],
                      start=(ch == 0), stop=(ch == NCH - 1))
    inc(te, sMM, _li)
    ve.wait_ge(sMM.sem, sMM.n)
    ve.wait_ge(sSc.sem, wbc_ready)
    inc(ve, sVe, ve.tensor_tensor(out=hw[:], in0=psB[:, 0:1024], in1=wbc[:],
                                  op=ALU.mult))
    hw_ready = sVe.n
    if stop_after == "s7":
        gp.wait_ge(sG.sem, sG.n)
        sync.wait_ge(sLd.sem, sLd.n)
        te.wait_ge(sMM.sem, sMM.n) if sMM.n else None
        sc.wait_ge(sSc.sem, sSc.n) if sSc.n else None
        ve.wait_ge(sVe.sem, sVe.n) if sVe.n else None
        st.close()
        return nc

    te.wait_ge(sVe.sem, hw_ready)
    gp.wait_ge(sMM.sem, sMM.n)
    funit = 0
    f_mm = {}
    f_ve = {}
    f_st = {}
    wb_ld = {}
    for ns in range(16):
        half = ns % 2
        dst = half * NCH * 256
        if ns >= 2:
            gp.wait_ge(sMM.sem, f_mm[("nsuser", ns - 2)])
        dma(gp, sG, wb2[:, dst:dst + NCH * 256],
            wbT[:, ns * NCH * 256:(ns + 1) * NCH * 256])
        wb_ld[ns] = sG.n
        te.wait_ge(sG.sem, wb_ld[ns])
        for tt in range(8):
            ps = PS6[funit % 6]
            if funit >= 6:
                te.wait_ge(sVe.sem, f_ve[funit - 6])
            for ch in range(NCH):
                te.matmul(ps[:, 0:256],
                          x1lt[:, ch * ROWS + tt * 128:ch * ROWS + (tt + 1) * 128],
                          wb2[:, dst + ch * 256:dst + (ch + 1) * 256],
                          start=(ch == 0), stop=False)
            te.matmul(ps[:, 0:256], ones_sb[:], bb_sb[0:1, ns * 256:(ns + 1) * 256],
                      start=False, stop=False)
            _li = te.matmul(ps[:, 0:256], hw[:, tt * 128:(tt + 1) * 128],
                      bstk_sb[:, ns * 256:(ns + 1) * 256], start=False, stop=True)
            inc(te, sMM, _li)
            f_mm[funit] = sMM.n
            xb = funit % 2
            if funit >= 2:
                sync.wait_ge(sVe.sem, f_ve[funit - 2])
            dma(sync, sX1C, x1st[:, xb * 512:xb * 512 + 256],
                x1_d[tt * 128:(tt + 1) * 128, ns * 256:(ns + 1) * 256])
            ve.wait_ge(sX1C.sem, sX1C.n)
            ve.wait_ge(sMM.sem, f_mm[funit])
            ob = funit % 2
            if funit >= 2:
                ve.wait_ge(sSt.sem, f_st[funit - 2])
            inc(ve, sVe, ve.tensor_tensor(out=outst[:, ob * 512:ob * 512 + 256],
                                          in0=ps[:, 0:256],
                                          in1=x1st[:, xb * 512:xb * 512 + 256],
                                          op=ALU.add))
            f_ve[funit] = sVe.n
            gp.wait_ge(sVe.sem, f_ve[funit])
            gp.dma_start(out=y[tt * 128:(tt + 1) * 128, ns * 256:(ns + 1) * 256],
                         in_=outst[:, ob * 512:ob * 512 + 256]).then_inc(sSt.sem, 16)
            sSt.n += 16
            f_st[funit] = sSt.n
            funit += 1
        f_mm[("nsuser", ns)] = sMM.n

    gp.wait_ge(sSt.sem, sSt.n)
    gp.wait_ge(sG.sem, sG.n)
    sync.wait_ge(sLd.sem, sLd.n)

    st.close()
    return nc


def _host_inputs(inputs):
    x = np.asarray(inputs["L0_attn_hidden_states"])
    qq = np.ascontiguousarray(np.asarray(inputs["L0_attn_qw_q"]))
    qk = np.ascontiguousarray(np.asarray(inputs["L0_attn_qw_k"]))
    qv = np.ascontiguousarray(np.asarray(inputs["L0_attn_qw_v"]))
    qo = np.ascontiguousarray(np.asarray(inputs["L0_attn_qw_o"]))
    scq = np.ascontiguousarray(np.asarray(inputs["L0_attn_scales_q"]))
    sck = np.ascontiguousarray(np.asarray(inputs["L0_attn_scales_k"]))
    scv = np.ascontiguousarray(np.asarray(inputs["L0_attn_scales_v"]))
    sco = np.ascontiguousarray(np.asarray(inputs["L0_attn_scales_o"]))
    wb = np.asarray(inputs["L0_ffn_W_base"])
    bbv = np.ascontiguousarray(np.asarray(inputs["L0_ffn_b_base"]).reshape(1, D))
    wr = np.asarray(inputs["L0_ffn_W_router"])
    A = np.asarray(inputs["L0_ffn_A"])
    B = np.asarray(inputs["L0_ffn_B"])

    wbT0 = wb.T
    wbT = np.ascontiguousarray(
        wbT0.reshape(32, 128, 16, 256).transpose(1, 2, 0, 3).reshape(128, 16 * 32 * 256))
    wrT = np.ascontiguousarray(
        wr.T.reshape(32, 128, 8).transpose(1, 0, 2).reshape(128, 256))
    aT = np.ascontiguousarray(
        A.reshape(128, D).T.reshape(32, 128, 128).transpose(1, 0, 2).reshape(128, D))
    bstack = np.ascontiguousarray(B.transpose(0, 2, 1).reshape(128, D))

    inv = 1.0 / (10000.0 ** (np.arange(0, HD, 2, dtype=np.float64) / HD))
    invf = np.concatenate([inv, inv])
    pos = np.arange(S, dtype=np.float64)
    ang = invf[:, None] * pos[None, :]
    cos_kv = np.cos(ang).astype(np.float32)
    sin_kv = np.sin(ang).astype(np.float32)
    iden_b = np.eye(128, dtype=ml_dtypes.bfloat16)
    iden32 = np.eye(128, dtype=np.float32)
    e2mv = (np.kron(np.eye(8), np.ones((1, 16))) * 2.0).astype(np.float32)
    ones_r = np.ones((1, 128), dtype=ml_dtypes.bfloat16)
    ones32_r = np.ones((1, 128), dtype=np.float32)
    tri = np.triu(np.ones((128, 128), dtype=np.float32))

    shared = dict(qw_q=qq, qw_k=qk, qw_v=qv, qw_o=qo, sc_q=scq, sc_k=sck,
                  sc_v=scv, sc_o=sco, wbT=wbT, bb=bbv, wrT=wrT, aT=aT,
                  bstack=bstack, cos_kv=cos_kv, sin_kv=sin_kv,
                  iden=iden_b, iden32=iden32, e2m=e2mv, ones_r=ones_r,
                  ones32_r=ones32_r)
    in_maps = []
    for c in range(8):
        b, cp = c // 4, c % 4
        rows = np.concatenate([np.arange((4 * s + cp) * 128, (4 * s + cp) * 128 + 128)
                               for s in range(NSLOT)])
        x_qv = np.ascontiguousarray(x[b][rows, :])
        x_kvv = np.ascontiguousarray(x[b])
        cos_qv = np.ascontiguousarray(cos_kv[:, rows])
        sin_qv = np.ascontiguousarray(sin_kv[:, rows])
        mk = np.zeros((4, 128, 512), dtype=np.float32)
        for p in range(4):
            if p < cp:
                mk[p] = 1.0
            elif p == cp:
                mk[p] = np.tile(tri, (1, 4))
        in_maps.append(dict(shared, x_kv=x_kvv, x_q=x_qv, cos_q=cos_qv,
                            sin_q=sin_qv, masks=mk.astype(ml_dtypes.bfloat16),
                            mypid=np.full((1, 1), c, np.uint32)))
    return in_maps


def kernel(**inputs):
    in_maps = _host_inputs(inputs)
    nc = build_nc()
    res = run_bass_kernel_spmd(nc, in_maps, core_ids=list(range(8)))
    out = np.empty((2, S, D), dtype=np.float32)
    for c in range(8):
        b, cp = c // 4, c % 4
        yc = np.asarray(res.results[c]["y"])
        for s in range(NSLOT):
            out[b, (4 * s + cp) * 128:(4 * s + cp) * 128 + 128, :] = \
                yc[s * 128:(s + 1) * 128, :]
    return out
